# version 38
# baseline (speedup 1.0000x reference)
from contextlib import ExitStack

import numpy as np
import ml_dtypes

import concourse.bass as bass
import concourse.mybir as mybir
from concourse.bass_utils import run_bass_kernel_spmd

F32 = mybir.dt.float32
BF16 = mybir.dt.bfloat16
ALU = mybir.AluOpType
ACT = mybir.ActivationFunctionType
AXL = mybir.AxisListType
bfdt = ml_dtypes.bfloat16

NCORES = 8
E = 1024
H = 16
D = 64
HD = H * D
C = 128
CLAMP = 76.0
GN_EPS = 0.001
LN_EPS = 1e-5
THRESH = [0.2441942809699397, 0.46396913384288535, 0.6861859295255305,
          0.9059607823984762, 1.083978413225562]
MATNAMES = ["CL128", "CL64", "CL32", "CL16", "SUF8F", "ASUF8", "PRE128", "SUF128"]


def make_matrices():
    j = np.arange(C)[:, None]
    p = np.arange(C)[None, :]

    def M_pre(B):
        return (((p // B) * B <= j) & (j < p)).astype(np.float32)

    def M_suf(B):
        return ((j >= p + 1) & (j < (p // B + 1) * B)).astype(np.float32)

    def M_asuf8():
        return -((j >= p) & (j < (p // 8 + 1) * 8)).astype(np.float32)

    def CL(B):
        return np.where((p % B) < (B // 2), M_suf(B // 2), M_pre(B // 2))

    mats = dict(CL128=CL(128), CL64=CL(64), CL32=CL(32), CL16=CL(16),
                SUF8F=M_suf(8), ASUF8=M_asuf8(), PRE128=M_pre(128), SUF128=M_suf(128))
    return np.concatenate([-mats[m] for m in MATNAMES], 0)


def make_masks():
    s = np.arange(C)[:, None]
    i = np.arange(C)[None, :]
    ML32 = ((s // 32 == i // 32) & (s % 32 < 16) & (i % 32 >= 16))
    ML16 = ((s // 16 == i // 16) & (s % 16 < 8) & (i % 16 >= 8))
    MD8 = ((s // 8 == i // 8) & (s < i))
    return np.concatenate([ML32, ML16, MD8], 0).astype(np.float32)


class Prog:
    def __init__(self, nc, stack):
        self.nc = nc
        self.stack = stack
        names = ("t", "v", "s", "g", "tn")
        self.sems = {e: stack.enter_context(nc.semaphore(f"sem_{e}")) for e in names}
        self.cnt = {e: 0 for e in names}
        self.waited = {}
        self.wr = {}
        self.rd = {}
        self._gr = []
        self.gsem = {}
        self.gcnt = {}

    def eng(self, e):
        n = self.nc
        return {"t": n.tensor, "v": n.vector, "s": n.scalar, "g": n.gpsimd,
                "tn": n.tensor, "q": n.sync, "gq": n.gpsimd}[e]

    def _wait(self, eng, peng, val):
        if isinstance(peng, tuple):
            grp = peng[1]
            val = self.gcnt[grp]
            key = (eng, peng)
            if self.waited.get(key, -1) >= val:
                return
            self.eng(eng).wait_ge(self.gsem[grp], val)
            self.waited[key] = val
            return
        key = (eng, peng)
        if self.waited.get(key, -1) >= val:
            return
        self.eng(eng).wait_ge(self.sems[peng], val)
        self.waited[key] = val

    def need(self, eng, keys):
        for k in keys:
            t = self.wr.get(k)
            if t is not None:
                self._wait(eng, t[0], t[1])

    def need_w(self, eng, keys):
        self.need(eng, keys)
        for k in keys:
            for peng, v in self.rd.get(k, {}).items():
                self._wait(eng, peng, v)

    def did(self, instr, eng, writes, reads):
        self.cnt[eng] += 1
        instr.then_inc(self.sems[eng], 1)
        v = self.cnt[eng]
        for k in writes:
            self.wr[k] = (eng, v)
            self.rd[k] = {}
        for k in reads:
            self.rd.setdefault(k, {})[eng] = v

    def dma(self, qe, out, in_, w, r=(), grp=None):
        if grp is None:
            grp = w if isinstance(w, str) else w[0]
            grp = qe + "".join(c for c in grp if c.isalpha())
        wt = w if isinstance(w, tuple) else (w,)
        self.need_w(qe, wt)
        self.need(qe, r)
        if grp not in self.gsem:
            self.gsem[grp] = self.stack.enter_context(self.nc.semaphore(f"gd_{grp}"))
            self.gcnt[grp] = 0
        i = self.eng(qe).dma_start(out=out, in_=in_)
        self.gcnt[grp] += 16
        i.then_inc(self.gsem[grp], 16)
        v = self.gcnt[grp]
        tok = ("dma", grp)
        for k in wt:
            self.wr[k] = (tok, v)
            self.rd[k] = {}
        for k in r:
            self.rd.setdefault(k, {})[tok] = v

    def mm(self, out, lhsT, rhs, w, r, start=True, stop=True, tile_position=None,
           skip_group_check=False, midsync=False):
        self.need("t", r)
        if start:
            self.need_w("t", (w,))
        kw = {}
        if tile_position is not None:
            kw["tile_position"] = tile_position
        if skip_group_check:
            kw["skip_group_check"] = True
        i = self.nc.tensor.matmul(out, lhsT, rhs, start=start, stop=stop, **kw)
        self._gr.extend(r)
        if stop:
            gr, self._gr = self._gr, []
            self.did(i, "t", (w,), gr)
        elif midsync:
            self.cnt["tn"] += 1
            self.nc.tensor.nop().then_inc(self.sems["tn"], 1)
            for k in r:
                self.rd.setdefault(k, {})["tn"] = self.cnt["tn"]


    def op(self, eng, fn, w, r):
        wt = w if isinstance(w, tuple) else (w,)
        self.need_w(eng, wt)
        self.need(eng, r)
        i = fn(self.eng(eng))
        self.did(i, eng, wt, r)

    def barrier(self):
        cur = dict(self.cnt)
        gcur = dict(self.gcnt)
        for e in ("t", "v", "s", "g"):
            for pe, v in cur.items():
                if v > 0 and pe != e:
                    self._wait(e, pe, v)
            for g, v in gcur.items():
                if v > 0:
                    self._wait(e, ("dma", g), v)
        for g, v in gcur.items():
            if v > 0:
                self._wait("q", ("dma", g), v)

    def finish(self, *qengs):
        for g, v in self.gcnt.items():
            self.eng("q").wait_ge(self.gsem[g], v)


def build_p1(T):
    NT = max(1, T // 512)
    TS = T // NT
    CPT = TS // C
    NCH = T // C
    nc = bass.Bass("TRN2", target_bir_lowering=False, debug=False)
    dt = nc.dram_tensor
    io = {}
    io["xT"] = dt("xT", [E, T + 1], F32, kind="ExternalInput").ap()
    io["laA"] = dt("laA", [E, 3 * 128], F32, kind="ExternalInput").ap()
    io["laAb"] = dt("laAb", [128, 3], F32, kind="ExternalInput").ap()
    io["laB"] = dt("laB", [3 * 128, E], F32, kind="ExternalInput").ap()
    io["mhdBD"] = dt("mhdBD", [5 * 8 * 128, 128], F32, kind="ExternalInput").ap()
    io["wkvABD"] = dt("wkvABD", [8 * 128, 128], F32, kind="ExternalInput").ap()
    io["wkvBBD"] = dt("wkvBBD", [8 * 128, 128], F32, kind="ExternalInput").ap()
    io["lambP"] = dt("lambP", [1, 8 * 128], F32, kind="ExternalInput").ap()
    io["CLmats"] = dt("CLmats", [8 * 128, 128], F32, kind="ExternalInput").ap()
    io["masksC"] = dt("masksC", [3 * 128, 128], F32, kind="ExternalInput").ap()
    io["muX"] = dt("muX", [128, 8], F32, kind="ExternalInput").ap()
    io["onesC"] = dt("onesC", [128, 4], F32, kind="ExternalInput").ap()
    io["onesR"] = dt("onesR", [1, 384], F32, kind="ExternalInput").ap()
    io["idnC"] = dt("idnC", [128, 128], F32, kind="ExternalInput").ap()
    io["o_rwkv"] = dt("o_rwkv", [T, HD], BF16, kind="ExternalOutput").ap()
    io["o_qglob"] = dt("o_qglob", [T, HD], BF16, kind="ExternalOutput").ap()
    io["o_rfm"] = dt("o_rfm", [E, T], BF16, kind="ExternalOutput").ap()
    io["o_dd5"] = dt("o_dd5", [E, T], BF16, kind="ExternalOutput").ap()
    io["o_sig"] = dt("o_sig", [128, 512], F32, kind="ExternalOutput").ap()
    io["o_qtot"] = dt("o_qtot", [1, HD], F32, kind="ExternalOutput").ap()

    st = ExitStack()
    with st:
        sb = lambda n_, s_, d_: st.enter_context(nc.sbuf_tensor(n_, s_, d_))
        psm = lambda n_, s_, d_=F32: st.enter_context(nc.psum_tensor(n_, s_, d_))
        B = {}
        B["laA"] = sb("s_laA", [128, 8 * 384], BF16)
        B["laAb"] = sb("s_laAb", [128, 3], F32)
        for p in range(3):
            B[f"laB{p}"] = sb(f"laB{p}", [128, E], BF16)
        B["mhd"] = sb("s_mhd", [128, 40 * 128], BF16)
        B["wkvA"] = sb("s_wkvA", [128, 8 * 128], BF16)
        B["wkvB"] = sb("s_wkvB", [128, 8 * 128], BF16)
        B["lamb"] = sb("s_lamb", [1, 8 * 128], BF16)
        B["mats"] = sb("s_mats", [128, 8 * 128], BF16)
        B["masks"] = sb("s_masks", [128, 3 * 128], BF16)
        B["mu"] = sb("s_mu", [128, 8], F32)
        B["onesc"] = sb("s_onesc", [128, 4], BF16)
        B["onesr"] = sb("s_onesr", [1, 384], BF16)
        B["idn"] = sb("s_idn", [128, 128], BF16)
        B["xt"] = sb("s_xt", [128, 8 * (TS + 1)], F32)
        B["diff"] = sb("s_diff", [128, 8 * TS], BF16)
        B["scr"] = sb("s_scr", [128, 8 * TS], BF16)
        B["zb"] = sb("s_zb", [128, 8 * TS], BF16)
        for p in range(3):
            B[f"hdn{p}"] = sb(f"hdn{p}", [128, TS], BF16)
        for p in range(2):
            B[f"ddg{p}"] = sb(f"ddg{p}", [128, 8 * TS], BF16)
        B["vfm"] = sb("s_vfm", [128, 8 * TS], BF16)
        B["rfm"] = sb("s_rfm", [128, 8 * TS], BF16)
        B["ktm"] = sb("s_ktm", [128, CPT * HD], BF16)
        B["vtm"] = sb("s_vtm", [128, CPT * HD], BF16)
        B["utm"] = sb("s_utm", [128, CPT * HD], BF16)
        B["rhotm"] = sb("s_rhotm", [128, CPT * HD], BF16)
        B["fac"] = sb("s_fac", [128, 8 * HD], BF16)
        B["Apl"] = sb("s_Apl", [128, 16 * 128], BF16)
        B["Amx"] = sb("s_Amx", [128, 16 * 128], BF16)
        B["kqb"] = sb("s_kqb", [128, 6 * HD], BF16)
        B["tmpb"] = sb("s_tmpb", [128, HD], BF16)
        B["ukb"] = sb("s_ukb", [128, HD], BF16)
        B["vrprod"] = sb("s_vrprod", [128, 8 * C], BF16)
        B["vrsb"] = sb("s_vrsb", [128, 16], BF16)
        B["sig"] = sb("s_sig", [128, 512], F32)
        B["sigsc"] = sb("s_sigsc", [128, 512], F32)
        B["sigb"] = sb("s_sigb", [128, 512], BF16)
        B["qcrow"] = sb("s_qcrow", [1, HD], BF16)
        B["rowsF"] = sb("s_rowsF", [128, 2 * HD], F32)
        B["onesrF"] = sb("s_onesrF", [1, 128], F32)
        B["rwkvo"] = sb("s_rwkvo", [128, 2 * HD], BF16)
        B["qglo"] = sb("s_qglo", [128, HD], BF16)
        for i in range(4):
            B[f"pb{i}"] = psm(f"pb{i}", [128, 512])
        B["pxl"] = psm("s_pxl", [128, 1024])
        B["pacc"] = psm("s_pacc", [128, 1024])

        sems = ExitStack()
        with sems, nc.Block() as blk:
            P = Prog(nc, sems)

            @blk.sync
            def _(_q):
                emit_p1(P, nc, io, B, T, NT, TS, CPT, NCH)
    return nc


def emit_p1(P, nc, io, B, T, NT, TS, CPT, NCH):
    laA, laAb = B["laA"], B["laAb"]
    laB = [B[f"laB{p}"] for p in range(3)]
    mhd, wkvA, wkvB, lamb = B["mhd"], B["wkvA"], B["wkvB"], B["lamb"]
    mats, masks, mu = B["mats"], B["masks"], B["mu"]
    onesc, onesr, idn = B["onesc"], B["onesr"], B["idn"]
    xt, diff, scr, zb = B["xt"], B["diff"], B["scr"], B["zb"]
    hdn = [B[f"hdn{p}"] for p in range(3)]
    ddg = [B["ddg0"], B["ddg1"]]
    vfm, rfm = B["vfm"], B["rfm"]
    whfm = B["scr"]
    wfm = B["zb"]
    ktm, vtm, utm, rhotm = B["ktm"], B["vtm"], B["utm"], B["rhotm"]
    fac, Apl = B["fac"], B["Apl"]
    Amx = B["Amx"]
    kqb, tmpb, ukb = B["kqb"], B["tmpb"], B["ukb"]
    vrprod, vrsb = B["vrprod"], B["vrsb"]
    sig, sigsc, sigb, qcrow = B["sig"], B["sigsc"], B["sigb"], B["qcrow"]
    rowsF = B["rowsF"]
    qtr = rowsF[0:1, 0:HD]
    rqr = rowsF[0:1, HD:2 * HD]
    onesrF, rwkvo, qglo = B["onesrF"], B["rwkvo"], B["qglo"]
    pb = [B[f"pb{i}"] for i in range(4)]
    pxl, pacc = B["pxl"], B["pacc"]

    def ld3(sbuf, hbm, wkey, csz=128):
        P.dma("gq", sbuf[:, :].rearrange("p (a c) -> p a c", c=csz),
              hbm.rearrange("(a p) c -> p a c", p=128), wkey)

    ld3(laA, io["laA"], "laA", 384)
    P.dma("q", laAb[:, :], io["laAb"][:, :], "laAb")
    for p in range(3):
        P.dma("gq", laB[p][:, :], io["laB"][p * 128:(p + 1) * 128, :], f"laB{p}")
    ld3(mhd, io["mhdBD"], "mhd")
    ld3(wkvA, io["wkvABD"], "wkvA")
    ld3(wkvB, io["wkvBBD"], "wkvB")
    P.dma("gq", lamb[:, :], io["lambP"][:, :], "lamb")
    ld3(mats, io["CLmats"], "mats")
    ld3(masks, io["masksC"], "masks")
    P.dma("q", mu[:, :], io["muX"][:, :], "mu")
    P.dma("gq", onesc[:, :], io["onesC"][:, :], "onesc")
    P.dma("gq", onesr[:, :], io["onesR"][:, :], "onesr")
    P.dma("q", onesrF[:, :], io["onesR"][0:1, 0:128], "onesrF")
    P.dma("gq", idn[:, :], io["idnC"][:, :], "idn")
    P.op("v", lambda e: e.memset(sig[:, :], 0.0), "sig", ())
    P.op("v", lambda e: e.memset(qtr, 1.0), "qtr", ())
    P.op("v", lambda e: e.memset(rqr, 1.0), "rqr", ())

    for tau in range(NT):
        t0 = tau * TS
        for c8 in range(8):
            P.dma("q", xt[:, c8 * (TS + 1):(c8 + 1) * (TS + 1)],
                  io["xT"][c8 * 128:(c8 + 1) * 128, t0:t0 + TS + 1], f"xt{c8}")
        for c8 in range(8):
            cur = xt[:, c8 * (TS + 1) + 1:(c8 + 1) * (TS + 1)]
            prv = xt[:, c8 * (TS + 1):(c8 + 1) * (TS + 1) - 1]
            df = diff[:, c8 * TS:(c8 + 1) * TS]
            sc = scr[:, c8 * TS:(c8 + 1) * TS]
            zc = zb[:, c8 * TS:(c8 + 1) * TS]
            P.op("v", lambda e, a=df, b=prv, c=cur: e.tensor_sub(a, b, c),
                 f"diff{c8}", (f"xt{c8}",))
            P.op("v", lambda e, a=sc, b=df, m=mu[:, c8:c8 + 1], x=cur:
                 e.scalar_tensor_tensor(a, b, m, x, ALU.mult, ALU.add),
                 f"scr{c8}", (f"diff{c8}", f"xt{c8}", "mu"))
            P.op("v", lambda e, a=sc, b=df: e.tensor_mul(a, b, a),
                 f"scr{c8}", (f"diff{c8}", f"scr{c8}"))
            P.op("v", lambda e, a=zc, b=sc, c=cur: e.tensor_add(a, b, c),
                 f"zb{c8}", (f"scr{c8}", f"xt{c8}"))
        for p in range(3):
            for c8 in range(8):
                P.mm(pb[p][:, 0:TS], laA[:, c8 * 384 + p * 128: c8 * 384 + (p + 1) * 128],
                     zb[:, c8 * TS:(c8 + 1) * TS], f"pb{p}", ("laA", f"zb{c8}"),
                     start=(c8 == 0), stop=(c8 == 7))
            P.op("s", lambda e, o=hdn[p][:, :], i=pb[p][:, 0:TS], b=laAb[:, p:p + 1]:
                 e.activation(o, i, ACT.Tanh, bias=b), f"hdn{p}", (f"pb{p}", "laAb"))
            P.op("v", lambda e, o=hdn[p][32:34, :]: e.memset(o, 1.0), f"hdn{p}", ())
            P.op("v", lambda e, o=hdn[p][96:98, :]: e.memset(o, 1.0), f"hdn{p}", ())
        for g in range(6):
            p, half = g // 2, g % 2
            rs = slice(64 * half, 64 * half + 34)
            dst = ddg[g % 2]
            dkey = f"ddg{g % 2}"
            for c8 in range(8):
                bi_ = c8 % 4
                bank, bkey = pb[bi_], f"pb{bi_}"
                P.mm(bank[:, 0:TS], laB[p][rs, c8 * 128:(c8 + 1) * 128],
                     hdn[p][rs, :], bkey, (f"laB{p}", f"hdn{p}"))
                sc = scr[:, c8 * TS:(c8 + 1) * TS]
                df = diff[:, c8 * TS:(c8 + 1) * TS]
                cur = xt[:, c8 * (TS + 1) + 1:(c8 + 1) * (TS + 1)]
                P.op("v", lambda e, a=sc, b=df, c=bank[:, 0:TS]: e.tensor_mul(a, b, c),
                     f"scr{c8}", (f"diff{c8}", bkey))
                P.op("g" if c8 % 2 == 0 else "v",
                     lambda e, a=dst[:, c8 * TS:(c8 + 1) * TS], b=sc, c=cur:
                     e.tensor_add(a, b, c), f"{dkey}_{c8}", (f"scr{c8}", f"xt{c8}"))
            if g == 5:
                for c8 in range(8):
                    P.dma("q", io["o_dd5"][c8 * 128:(c8 + 1) * 128, t0:t0 + TS],
                          dst[:, c8 * TS:(c8 + 1) * TS], f"odd5_{tau}_{c8}", (f"ddg1_{c8}",))
                continue
            gi = {0: "k", 1: "v", 2: "r", 3: "w", 4: "u"}[g]
            if gi in ("k", "v", "u"):
                dstt = {"k": ktm, "v": vtm, "u": utm}[gi]
                for pr in range(8):
                    for cc in range(CPT):
                        bi = (pr * CPT + cc) % 4
                        bank, bkey = pb[bi], f"pb{bi}"
                        P.mm(bank[:, 0:128],
                             dst[:, pr * TS + cc * C: pr * TS + (cc + 1) * C],
                             mhd[:, (g * 8 + pr) * 128:(g * 8 + pr + 1) * 128],
                             bkey, (f"{dkey}_{pr}", "mhd"))
                        osl = dstt[:, cc * HD + pr * 128: cc * HD + (pr + 1) * 128]
                        if (pr + cc) % 2 == 0:
                            P.op("v", lambda e, o=osl, i=bank[:, 0:128]: e.tensor_copy(o, i),
                                 f"{gi}tm{cc}_{pr}", (bkey,))
                        else:
                            P.op("s", lambda e, o=osl, i=bank[:, 0:128]: e.copy(o, i),
                                 f"{gi}tm{cc}_{pr}", (bkey,))
            if gi in ("v", "r", "w"):
                dstf = {"v": vfm, "r": rfm, "w": wfm}[gi]
                for pr in range(8):
                    bf_ = pr % 4
                    P.mm(pb[bf_][:, 0:TS], mhd[:, (g * 8 + pr) * 128:(g * 8 + pr + 1) * 128],
                         dst[:, pr * TS:(pr + 1) * TS], f"pb{bf_}", (f"{dkey}_{pr}", "mhd"))
                    dsl = dstf[:, pr * TS:(pr + 1) * TS]
                    if pr % 2 == 0:
                        P.op("v", lambda e, o=dsl, i=pb[bf_][:, 0:TS]: e.tensor_copy(o, i),
                             f"{gi}fm{pr}" if gi != "w" else f"zb{pr}", (f"pb{bf_}",))
                    else:
                        P.op("s", lambda e, o=dsl, i=pb[bf_][:, 0:TS]: e.copy(o, i),
                             f"{gi}fm{pr}" if gi != "w" else f"zb{pr}", (f"pb{bf_}",))
        for c8 in range(8):
            P.dma("q", io["o_rfm"][c8 * 128:(c8 + 1) * 128, t0:t0 + TS],
                  rfm[:, c8 * TS:(c8 + 1) * TS], f"orfm_{tau}_{c8}", (f"rfm{c8}",))
        for pr in range(8):
            bf_ = pr % 4
            P.mm(pb[bf_][:, 0:TS], wkvA[:, pr * 128:(pr + 1) * 128],
                 wfm[:, pr * TS:(pr + 1) * TS], f"pb{bf_}", (f"zb{pr}", "wkvA"))
            P.op("s", lambda e, o=whfm[:, pr * TS:(pr + 1) * TS], i=pb[bf_][:, 0:TS]:
                 e.activation(o, i, ACT.Tanh), f"scr{pr}", (f"pb{bf_}",))
        for pr in range(8):
            for cc in range(CPT):
                bi = (pr * CPT + cc) % 4
                bank, bkey = pb[bi], f"pb{bi}"
                P.mm(bank[:, 0:128], whfm[:, pr * TS + cc * C: pr * TS + (cc + 1) * C],
                     wkvB[:, pr * 128:(pr + 1) * 128], bkey,
                     (f"whfm{pr}", "wkvB"), start=True, stop=False)
                P.mm(bank[:, 0:128], onesr[0:1, 0:128], lamb[0:1, pr * 128:(pr + 1) * 128],
                     bkey, ("onesr", "lamb"), start=False, stop=True)
                P.op("s", lambda e, o=rhotm[:, cc * HD + pr * 128: cc * HD + (pr + 1) * 128],
                     i=bank[:, 0:128]: e.activation(o, i, ACT.Exp),
                     f"rhotm{cc}_{pr}", (bkey,))

        for cc in range(CPT):
            ch = tau * CPT + cc
            rkeys = [f"rhotm{cc}_{pr}" for pr in range(8)]
            for mpair in range(4):
                for mi in range(2):
                    m = mpair * 2 + mi
                    for hf in range(2):
                        bi = mi * 2 + hf
                        P.mm(pb[bi][:, :], mats[:, m * 128:(m + 1) * 128],
                             rhotm[:, cc * HD + hf * 512: cc * HD + (hf + 1) * 512],
                             f"pb{bi}", ["mats"] + rkeys)
                    if MATNAMES[m] == "ASUF8":
                        for hf in range(2):
                            bi = mi * 2 + hf
                            P.op("v", lambda e, o=pb[bi][:, :]:
                                 e.tensor_scalar_min(o, o, CLAMP), f"pb{bi}", (f"pb{bi}",))
                    for hf in range(2):
                        bi = mi * 2 + hf
                        P.op("s", lambda e, o=fac[:, m * HD + hf * 512: m * HD + (hf + 1) * 512],
                             i=pb[bi][:, :]: e.activation(o, i, ACT.Exp),
                             f"fac{m}", (f"pb{bi}",))
            for h in range(16):
                bi = h // 4
                rs = slice(64 * (h % 2), 64 * (h % 2) + 64)
                cols = slice((h // 2) * TS + cc * C, (h // 2) * TS + (cc + 1) * C)
                P.mm(pb[bi][:, (h % 4) * 128:(h % 4 + 1) * 128], vfm[rs, cols], rfm[rs, cols],
                     f"pb{bi}", (f"vfm{h // 2}", f"rfm{h // 2}"))
            for b in range(4):
                P.op("s", lambda e, o=Apl[:, b * 512:(b + 1) * 512], i=pb[b][:, :]:
                     e.copy(o, i), f"Apl{b}", (f"pb{b}",))
            aplk = ["Apl0", "Apl1", "Apl2", "Apl3"]
            ktmk = [f"ktm{cc}_{pr}" for pr in range(8)]
            for kqi, m in enumerate([0, 1, 2, 3, 4, 7]):
                P.op("v",
                     lambda e, o=kqb[:, kqi * HD:(kqi + 1) * HD],
                     a=ktm[:, cc * HD:(cc + 1) * HD], f=fac[:, m * HD:(m + 1) * HD]:
                     e.tensor_tensor(o, a, f, ALU.mult),
                     f"kq{kqi}", [f"fac{m}"] + ktmk)
            P.op("v", lambda e, o=ukb[:, :], a=utm[:, cc * HD:(cc + 1) * HD],
                 b=ktm[:, cc * HD:(cc + 1) * HD]: e.tensor_tensor(o, a, b, ALU.mult),
                 "ukb", [f"utm{cc}_{pr}" for pr in range(8)] + ktmk)
            v3o = vrprod[:, :].rearrange("p (a c) -> p a c", c=C)
            v3a = vfm[:, :].rearrange("p (a c) -> p a c", c=TS)[:, :, cc * C:(cc + 1) * C]
            v3b = rfm[:, :].rearrange("p (a c) -> p a c", c=TS)[:, :, cc * C:(cc + 1) * C]
            P.op("v", lambda e: e.tensor_tensor(v3o, v3a, v3b, ALU.mult), "vrprod",
                 [f"vfm{pr}" for pr in range(8)] + [f"rfm{pr}" for pr in range(8)])
            for pr in range(8):
                P.mm(pb[0][:, pr * 2:pr * 2 + 2], vrprod[:, pr * C:(pr + 1) * C],
                     onesc[:, 1:3], "pb0", ("vrprod", "onesc"))
            P.op("v", lambda e: e.tensor_copy(vrsb[:, :], pb[0][:, 0:16]), "vrsb", ("pb0",))
            t3o = tmpb[:, :].rearrange("p (h d) -> p h d", d=D)
            u3 = ukb[:, :].rearrange("p (h d) -> p h d", d=D)
            vr3 = vrsb[:, :].unsqueeze(2).broadcast_to([128, 16, D])
            P.op("v", lambda e: e.tensor_tensor(t3o, u3, vr3, ALU.mult), "tmpb", ("ukb", "vrsb"))
            P.mm(pacc[:, 0:512], idn[:, :], tmpb[:, 0:512], "pacc", ("idn", "tmpb"),
                 start=True, stop=False, skip_group_check=True)
            P.mm(pacc[:, 512:1024], idn[:, :], tmpb[:, 512:1024], "pacc", ("idn", "tmpb"),
                 start=True, stop=False, skip_group_check=True)

            def acc_scaled(lo, hi, xkey, fm):
                P.op("v", lambda e, o=tmpb[lo:hi, :], x=pxl[lo:hi, :],
                     f=fac[lo:hi, fm * HD:(fm + 1) * HD]:
                     e.tensor_tensor(o, x, f, ALU.mult), "tmpb", (xkey, f"fac{fm}"))
                tp = None if lo == 0 else (lo, lo)
                P.mm(pacc[lo:hi, 0:512], idn[lo:hi, lo:hi], tmpb[lo:hi, 0:512],
                     "pacc", ("idn", "tmpb"), start=False, stop=False,
                     tile_position=tp, skip_group_check=True)
                P.mm(pacc[lo:hi, 512:1024], idn[lo:hi, lo:hi], tmpb[lo:hi, 512:1024],
                     "pacc", ("idn", "tmpb"), start=False, stop=False,
                     tile_position=tp, skip_group_check=True)

            for h in range(16):
                P.mm(pxl[64:128, h * 64:(h + 1) * 64],
                     Apl[0:64, h * 128 + 64:(h + 1) * 128],
                     kqb[0:64, 0 * HD + h * 64: 0 * HD + (h + 1) * 64],
                     "pxl", (f"Apl{h // 4}", "kq0"), tile_position=(0, 64))
            acc_scaled(64, 128, "pxl", 0)
            for h in range(16):
                P.mm(pxl[32:64, h * 64:(h + 1) * 64],
                     Apl[0:32, h * 128 + 32: h * 128 + 64],
                     kqb[0:32, 1 * HD + h * 64: 1 * HD + (h + 1) * 64],
                     "pxl", (f"Apl{h // 4}", "kq1"), tile_position=(0, 32))
                P.mm(pxl[96:128, h * 64:(h + 1) * 64],
                     Apl[64:96, h * 128 + 96:(h + 1) * 128],
                     kqb[64:96, 1 * HD + h * 64: 1 * HD + (h + 1) * 64],
                     "pxl", (f"Apl{h // 4}", "kq1"), tile_position=(64, 96))
            acc_scaled(32, 64, "pxl", 1)
            acc_scaled(96, 128, "pxl", 1)
            for (ami, fm, kqi) in ((0, 2, 2), (1, 3, 3), (2, 5, 4)):
                a3 = Apl[:, :].rearrange("p (h c) -> p h c", c=128)
                m3 = masks[:, ami * 128:(ami + 1) * 128].unsqueeze(1).broadcast_to([128, 16, 128])
                o3 = Amx[:, :].rearrange("p (h c) -> p h c", c=128)
                P.op("v",
                     lambda e, o=o3, a=a3, m=m3: e.tensor_tensor(o, a, m, ALU.mult),
                     "Amx", aplk + ["masks"])
                for h in range(16):
                    P.mm(pxl[:, h * 64:(h + 1) * 64], Amx[:, h * 128:(h + 1) * 128],
                         kqb[:, kqi * HD + h * 64: kqi * HD + (h + 1) * 64],
                         "pxl", ("Amx", f"kq{kqi}"))
                acc_scaled(0, 128, "pxl", fm)
            P.op("v", lambda e: e.tensor_copy(sigb[:, :], sig[:, :]), "sigb", ("sig",))
            for h in range(16):
                rs = slice(64 * (h % 2), 64 * (h % 2) + 64)
                cols = slice((h // 2) * TS + cc * C, (h // 2) * TS + (cc + 1) * C)
                P.mm(pxl[:, h * 64:(h + 1) * 64], rfm[rs, cols],
                     sigb[rs, (h // 2) * 64:(h // 2) * 64 + 64],
                     "pxl", (f"rfm{h // 2}", "sigb"))
            P.op("v", lambda e, o=tmpb[:, :], x=pxl[:, :], f=fac[:, 6 * HD:7 * HD]:
                 e.tensor_tensor(o, x, f, ALU.mult), "tmpb", ("pxl", "fac6"))
            P.mm(pacc[:, 0:512], idn[:, :], tmpb[:, 0:512], "pacc", ("idn", "tmpb"),
                 start=False, stop=False, skip_group_check=True)
            P.mm(pacc[:, 512:1024], idn[:, :], tmpb[:, 512:1024], "pacc", ("idn", "tmpb"),
                 start=False, stop=True, skip_group_check=True)
            par = (ch % 2) * HD
            for hf in range(2):
                P.op("s", lambda e, o=rwkvo[:, par + hf * 512: par + (hf + 1) * 512],
                     i=pacc[:, hf * 512:(hf + 1) * 512]: e.copy(o, i),
                     f"rwkvo{ch % 2}", ("pacc",))
            P.dma("q", io["o_rwkv"][ch * 128:(ch + 1) * 128, :], rwkvo[:, par:par + HD],
                  f"orw{ch}", (f"rwkvo{ch % 2}",), grp="qorw")
            for hf in range(2):
                bi = 1 + hf
                P.mm(pb[bi][:, :], onesrF[0:1, 0:128],
                     rowsF[0:1, HD + hf * 512: HD + (hf + 1) * 512],
                     f"pb{bi}", ("onesrF", "rqr"))
                P.op("v", lambda e, o=qglo[:, hf * 512:(hf + 1) * 512],
                     f=fac[:, 6 * HD + hf * 512: 6 * HD + (hf + 1) * 512],
                     r=pb[bi][:, :]: e.tensor_tensor(o, f, r, ALU.mult),
                     "qglo", ("fac6", f"pb{bi}"))
            P.dma("q", io["o_qglob"][ch * 128:(ch + 1) * 128, :], qglo[:, :],
                  f"oqg{ch}", ("qglo",), grp="qoqg")
            for h in range(16):
                P.mm(pb[3][64 * (h % 2):64 * (h % 2) + 64, (h // 2) * 64:(h // 2) * 64 + 64],
                     vtm[:, cc * HD + h * 64: cc * HD + (h + 1) * 64],
                     kqb[:, 5 * HD + h * 64: 5 * HD + (h + 1) * 64],
                     "pb3", [f"vtm{cc}_{pr}" for pr in range(8)] + ["kq5"],
                     tile_position=(0, 64 * (h % 2)))
            for hf in range(2):
                P.mm(pb[2][0:1, :], onesc[:, 0:1],
                     rhotm[:, cc * HD + hf * 512: cc * HD + (hf + 1) * 512],
                     "pb2", ["onesc"] + rkeys)
                P.op("s", lambda e, o=qcrow[0:1, hf * 512:(hf + 1) * 512], i=pb[2][0:1, :]:
                     e.activation(o, i, ACT.Exp, scale=-1.0), "qcrow", ("pb2",))
            ev = qcrow[:, :].rearrange("o (a b) -> o a b", b=64)[:, 0::2, :]
            od = qcrow[:, :].rearrange("o (a b) -> o a b", b=64)[:, 1::2, :]
            P.mm(pb[1][:, :], onesr[0:1, 128:256], ev, "pb1", ("onesr", "qcrow"),
                 start=True, stop=False)
            P.mm(pb[1][:, :], onesr[0:1, 256:384], od, "pb1", ("onesr", "qcrow"),
                 start=False, stop=True)
            P.op("v", lambda e: e.tensor_tensor(sigsc[:, :], sig[:, :], pb[1][:, :], ALU.mult),
                 "sigsc", ("sig", "pb1"))
            P.op("v", lambda e: e.tensor_tensor(sig[:, :], sigsc[:, :], pb[3][:, :], ALU.add),
                 "sig", ("sigsc", "pb3"))
            P.op("v", lambda e: e.tensor_tensor(qtr, qtr, qcrow[:, :], ALU.mult),
                 "qtr", ("qtr", "qcrow"))
            P.op("v", lambda e: e.tensor_tensor(rqr, rqr, qcrow[:, :], ALU.mult),
                 "rqr", ("rqr", "qcrow"))

    P.dma("q", io["o_sig"][:, :], sig[:, :], "osig", ("sig",))
    P.dma("q", io["o_qtot"][:, :], qtr, "oqt", ("qtr",))
    if not io.get("fused"):
        P.finish()


def build_p2(T):
    NTT = max(1, T // 512)
    TS = T // NTT
    CPT = TS // C
    FT = min(T, 512)
    NFT = T // FT
    FO = 4096
    nc = bass.Bass("TRN2", target_bir_lowering=False, debug=False)
    dt = nc.dram_tensor
    io = {}
    io["xT"] = dt("xT", [E, T + 1], F32, kind="ExternalInput").ap()
    io["rwkv"] = dt("rwkv", [T, HD], BF16, kind="ExternalInput").ap()
    io["qglob"] = dt("qglob", [T, HD], BF16, kind="ExternalInput").ap()
    io["rfmH"] = dt("rfmH", [E, T], BF16, kind="ExternalInput").ap()
    io["dd5H"] = dt("dd5H", [E, T], BF16, kind="ExternalInput").ap()
    io["sigAll"] = dt("sigAll", [7 * 128, 512], F32, kind="ExternalInput").ap()
    io["qtAll"] = dt("qtAll", [7, HD], F32, kind="ExternalInput").ap()
    io["gateW"] = dt("gateW", [E, E], F32, kind="ExternalInput").ap()
    io["outW"] = dt("outW", [E, E], F32, kind="ExternalInput").ap()
    io["outB"] = dt("outB", [128, 8], F32, kind="ExternalInput").ap()
    io["gnGB"] = dt("gnGB", [1, 2 * HD], F32, kind="ExternalInput").ap()
    io["WfcT"] = dt("WfcT", [E, FO], F32, kind="ExternalInput").ap()
    io["WprojT"] = dt("WprojT", [FO, E], F32, kind="ExternalInput").ap()
    io["onesC"] = dt("onesC", [128, 4], F32, kind="ExternalInput").ap()
    io["onesR"] = dt("onesR", [1, 384], F32, kind="ExternalInput").ap()
    io["idnC"] = dt("idnC", [128, 128], F32, kind="ExternalInput").ap()
    io["o_out"] = dt("o_out", [E, T], F32, kind="ExternalOutput").ap()

    st = ExitStack()
    with st:
        sb = lambda n_, s_, d_: st.enter_context(nc.sbuf_tensor(n_, s_, d_))
        psm = lambda n_, s_, d_=F32: st.enter_context(nc.psum_tensor(n_, s_, d_))
        B = {}
        B["wfc"] = sb("s_wfc", [128, 8 * FO], BF16)
        B["onesc"] = sb("s_onesc", [128, 4], BF16)
        B["onesr"] = sb("s_onesr", [1, 384], BF16)
        B["onesrF"] = sb("s_onesrF", [1, 128], F32)
        B["out1"] = sb("s_out1", [128, 8 * T], BF16)
        for i in range(6):
            B[f"pb{i}"] = psm(f"s_pb{i}", [128, 512])
        B["pst"] = psm("s_pst", [1, 512])
        B["pst2"] = psm("s_pst2", [1, 512])

        sems = ExitStack()
        with sems, nc.Block() as blk:
            P = Prog(nc, sems)

            @blk.sync
            def _(_q):
                ast = ExitStack()
                with ast:
                    sba = lambda n_, s_, d_: ast.enter_context(nc.sbuf_tensor(n_, s_, d_))
                    B["gate"] = sba("s_gate", [128, 8 * E], BF16)
                    B["outw"] = sba("s_outw", [128, 8 * E], BF16)
                    B["outb"] = sba("s_outb", [128, 8], F32)
                    B["gnrep"] = sba("s_gnrep", [128, 2 * HD], BF16)
                    B["sigI"] = sba("s_sigI", [128, 512], F32)
                    B["sigB"] = sba("s_sigB", [128, 512], BF16)
                    B["xt"] = sba("s_xt", [128, 8 * TS], BF16)
                    B["rwkvc"] = sba("s_rwkvc", [128, HD], F32)
                    B["rwkvb"] = [sba("s_rwkvb" + str(i2), [128, HD], BF16)
                                      for i2 in range(2)]
                    B["qgl"] = [sba("s_qgl" + str(i2), [128, HD], BF16)
                                      for i2 in range(2)]
                    B["stats"] = sba("s_stats", [128, 128], F32)
                    B["nrm"] = sba("s_nrm", [128, HD], F32)
                    B["nrmb"] = sba("s_nrmb", [128, HD], BF16)
                    B["gsil"] = sba("s_gsil", [128, HD], BF16)
                    B["prod"] = [sba("s_prod" + str(i2), [128, HD], F32)
                                      for i2 in range(2)]
                    B["idnF"] = sba("s_idnF", [128, 128], F32)
                    B["idnB"] = sba("s_idnB", [128, 128], BF16)
                    B["prodT"] = sba("s_prodT", [128, 8 * TS], BF16)
                    B["dd5t"] = sba("s_dd5t", [128, 8 * TS], BF16)
                    B["rfmt"] = sba("s_rfmt", [128, 8 * TS], BF16)
                    B["vres"] = sba("s_vres", [128, 8 * TS], BF16)
                    B["rowst"] = sba("s_rowst", [128, TS], F32)
                    emit_p2_attn(P, nc, io, B, T, NTT, TS, CPT, FO)
                P.barrier()
                fst = ExitStack()
                with fst:
                    sbf = lambda n_, s_, d_: fst.enter_context(nc.sbuf_tensor(n_, s_, d_))
                    B["wpj"] = sbf("s_wpj", [128, 32 * E], BF16)
                    B["hbuf"] = sbf("s_hbuf", [128, 32 * FT], BF16)
                    B["relu"] = sbf("s_relu", [128, FT], F32)
                    B["sacc"] = sbf("s_sacc", [128, FT], BF16)
                    B["vres2"] = sbf("s_vres2", [128, 8 * FT], BF16)
                    B["rowst2"] = sbf("s_rowst2", [128, FT], F32)

                    emit_p2_ffn(P, nc, io, B, T, NTT, TS, FT, NFT, FO)
    return nc


def emit_p2_attn(P, nc, io, B, T, NTT, TS, CPT, FO):
    gate, outw, outb = B["gate"], B["outw"], B["outb"]
    wfc = B["wfc"]
    onesc, onesr = B["onesc"], B["onesr"]
    gnrep = B["gnrep"]
    sigI, sigB = B["sigI"], B["sigB"]
    xt, rwkvc, rwkvb, qgl = B["xt"], B["rwkvc"], B["rwkvb"], B["qgl"]
    tmpc = B["gsil"]
    stats, nrm, nrmb = B["stats"], B["nrm"], B["nrmb"]
    sq = nrm
    gsil, prod, prodT = B["gsil"], B["prod"], B["prodT"]
    dd5t, rfmt, vres = B["dd5t"], B["rfmt"], B["vres"]
    rowst = B["rowst"]
    out1 = B["out1"]
    pb = [B[f"pb{i}"] for i in range(6)]
    pst, pst2 = B["pb4"], B["pb5"]

    def ld3(sbuf, hbm, wkey, csz):
        P.dma("gq", sbuf[:, :].rearrange("p (a c) -> p a c", c=csz),
              hbm.rearrange("(a p) c -> p a c", p=128), wkey)

    ld3(gate, io["gateW"], "gate", E)
    ld3(outw, io["outW"], "outw", E)
    P.dma("q", outb[:, :], io["outB"][:, :], "outb")
    P.dma("gq", onesc[:, :], io["onesC"][:, :], "onesc")
    P.dma("gq", onesr[:, :], io["onesR"][:, :], "onesr")
    P.dma("q", B["onesrF"][:, :], io["onesR"][0:1, 0:128], "onesrF")
    P.dma("q", B["idnF"][:, :], io["idnC"][:, :], "idnF")
    P.dma("gq", B["idnB"][:, :], io["idnC"][:, :], "idn")
    gnrow = B["rwkvc"][0:1, :].bitcast(BF16)
    P.dma("gq", gnrow, io["gnGB"][:, :], ("rwcA", "rwcB"))
    for c8 in range(8):
        P.dma("gq", xt[:, c8 * TS:(c8 + 1) * TS],
              io["xT"][c8 * 128:(c8 + 1) * 128, 1:TS + 1], f"xt{c8}")
        P.dma("q", dd5t[:, c8 * TS:(c8 + 1) * TS],
              io["dd5H"][c8 * 128:(c8 + 1) * 128, 0:TS], f"dd5t{c8}")
        P.dma("q", rfmt[:, c8 * TS:(c8 + 1) * TS],
              io["rfmH"][c8 * 128:(c8 + 1) * 128, 0:TS], f"rfmt{c8}")
    if io.get("collectives"):
        io["collectives"]()
    for gb in range(2):
        for hf in range(2):
            P.mm(pb[4][:, :], onesr[0:1, 0:128],
                 gnrow[0:1, gb * HD + hf * 512: gb * HD + (hf + 1) * 512],
                 "pb4", ("onesr", "rwcA", "rwcB"))
            P.op("v", lambda e, o=gnrep[:, gb * HD + hf * 512: gb * HD + (hf + 1) * 512],
                 i=pb[4][:, :]: e.tensor_copy(o, i), "gnrep", ("pb4",))
    sigT = rwkvc[:, 0:512]
    sigJs = [(rwkvc[:, 512:1024], "rwcB"), (nrm[:, 512:1024], "nrm")]
    qts = [(qgl[0][0:1, 0:HD], "qgl0"), (qgl[1][0:1, 0:HD], "qgl1")]
    P.op("v", lambda e: e.memset(sigI[:, :], 0.0), "sigI", ())
    nslots = NCORES if io.get("fused") else 7
    fused = bool(io.get("fused"))

    def fold_load(j):
        p = j % 2
        sj, sjk = sigJs[p]
        qt, qtk = qts[p]
        if fused:
            P.dma("q", sj, io["sigAll"][j * 128:(j + 1) * 128, :], sjk,
                  r=("ccsigG",), grp=f"qsj{p}")
            P.dma("gq", qt, io["qtAll"][j:j + 1, :], qtk, r=("ccqtG",),
                  grp=f"gqt{p}")
        else:
            P.dma("q", sj, io["sigAll"][j * 128:(j + 1) * 128, :], sjk,
                  grp=f"qsj{p}")
            P.dma("gq", qt, io["qtAll"][j:j + 1, :], qtk, grp=f"gqt{p}")

    fold_load(0)
    for jj in range(nslots):
        p = jj % 2
        sj, sjk = sigJs[p]
        qt, qtk = qts[p]
        if jj + 1 < nslots:
            fold_load(jj + 1)
        if fused:
            cm = B["cmask"]
            P.op("v", lambda e, o=sj: e.tensor_single_scalar(o, o, cm[:, jj:jj + 1],
                                                             ALU.mult), sjk, (sjk, "cmask"))
            P.op("v", lambda e, o=qt: e.tensor_scalar(o, o, cm[0:1, jj:jj + 1],
                                                      cm[0:1, 8 + jj:9 + jj], ALU.mult, ALU.add),
                 qtk, (qtk, "cmask"))
        ev = qt.rearrange("o (a b) -> o a b", b=64)[:, 0::2, :]
        od = qt.rearrange("o (a b) -> o a b", b=64)[:, 1::2, :]
        P.mm(pb[5][:, :], onesr[0:1, 128:256], ev, "pb5", ("onesr", qtk),
             start=True, stop=False)
        P.mm(pb[5][:, :], onesr[0:1, 256:384], od, "pb5", ("onesr", qtk),
             start=False, stop=True)
        P.op("v", lambda e: e.tensor_tensor(sigT, sigI[:, :], pb[5][:, :], ALU.mult),
             "rwcA", ("sigI", "pb5"))
        P.op("v", lambda e, a=sj: e.tensor_tensor(sigI[:, :], sigT, a, ALU.add),
             "sigI", ("rwcA", sjk))
    P.op("v", lambda e: e.tensor_copy(sigB[:, :], sigI[:, :]), "sigB", ("sigI",))

    def chunk_front(tau, cc):
        p = cc % 2
        rwb, qg = rwkvb[p], qgl[p]
        rwc, tmp = rwkvc, tmpc
        st, nr, nrb = stats, nrm, nrmb
        sq_, gs, pr = sq, gsil, prod[p]
        ch = tau * CPT + cc
        P.dma("q", rwb[:, :], io["rwkv"][ch * 128:(ch + 1) * 128, :], f"rwb{p}",
              grp=f"qrwb{p}")
        P.dma("q", qg[:, :], io["qglob"][ch * 128:(ch + 1) * 128, :], f"qgl{p}",
              grp=f"qqgl{p}")
        for h in range(16):
            bi = 2 + h // 8
            rs = slice(64 * (h % 2), 64 * (h % 2) + 64)
            cols = slice((h // 2) * TS + cc * C, (h // 2) * TS + (cc + 1) * C)
            P.mm(pb[bi][:, (h % 8) * 64:(h % 8) * 64 + 64], rfmt[rs, cols],
                 sigB[rs, (h // 2) * 64:(h // 2) * 64 + 64],
                 f"pb{bi}", (f"rfmt{h // 2}", "sigB"))
        for b in range(2):
            P.op("v", lambda e, o=tmp[:, b * 512:(b + 1) * 512], x=pb[2 + b][:, :],
                 q=qg[:, b * 512:(b + 1) * 512]: e.tensor_tensor(o, x, q, ALU.mult),
                 "gsil", (f"pb{2 + b}", f"qgl{p}"))
        P.op("v", lambda e: e.tensor_tensor(rwc[:, :], rwb[:, :], tmp[:, :], ALU.add),
             ("rwc", "rwcA", "rwcB"), (f"rwb{p}", "gsil"))
        r3 = rwc[:, :].rearrange("p (h d) -> p h d", d=D)
        P.op("v", lambda e, o=st[:, 0:16]: e.tensor_reduce(o, r3, AXL.X, ALU.add),
             "st_sum", ("rwc",))
        P.op("v", lambda e: e.tensor_tensor(sq_[:, :], rwc[:, :], rwc[:, :], ALU.mult),
             "nrm", ("rwc",))
        s3 = sq_[:, :].rearrange("p (h d) -> p h d", d=D)
        P.op("v", lambda e, o=st[:, 16:32]: e.tensor_reduce(o, s3, AXL.X, ALU.add),
             "st_sq", ("nrm",))
        P.op("v", lambda e, o=st[:, 32:48], i=st[:, 0:16]:
             e.tensor_scalar_mul(o, i, 1.0 / D), "st_m", ("st_sum",))
        P.op("v", lambda e, o=st[:, 48:64], i=st[:, 16:32]:
             e.tensor_scalar_mul(o, i, 1.0 / D), "st_msq", ("st_sq",))
        P.op("v", lambda e, o=st[:, 64:80], a=st[:, 32:48], b=st[:, 32:48]:
             e.tensor_tensor(o, a, b, ALU.mult), "st_m2", ("st_m",))
        P.op("v", lambda e, o=st[:, 80:96], a=st[:, 48:64], b=st[:, 64:80]:
             e.tensor_tensor(o, a, b, ALU.subtract), "st_var",
             ("st_msq", "st_m2"))
        P.op("v", lambda e, o=st[:, 80:96], i=st[:, 80:96]:
             e.tensor_scalar_add(o, i, GN_EPS), "st_var", ("st_var",))
        P.op("s", lambda e, o=st[:, 96:112], i=st[:, 80:96]:
             e.activation(o, i, ACT.Sqrt), "st_sd", ("st_var",))
        P.op("v", lambda e, o=st[:, 112:128], i=st[:, 96:112]:
             e.reciprocal(o, i), "st_rstd", ("st_sd",))
        m3 = st[:, 32:48].unsqueeze(2).broadcast_to([128, 16, D])
        rs3 = st[:, 112:128].unsqueeze(2).broadcast_to([128, 16, D])
        n3 = nr[:, :].rearrange("p (h d) -> p h d", d=D)
        P.op("v", lambda e: e.tensor_tensor(n3, r3, m3, ALU.subtract), "nrm",
             ("rwc", "st_m"))
        P.op("v", lambda e: e.tensor_tensor(n3, n3, rs3, ALU.mult), "nrm",
             ("nrm", "st_rstd"))
        P.op("v", lambda e, o=nr[:, :]: e.tensor_tensor(o, o, gnrep[:, 0:HD], ALU.mult),
             "nrm", ("nrm", "gnrep"))
        P.op("v", lambda e, o=nrb[:, :], i=nr[:, :]:
             e.tensor_tensor(o, i, gnrep[:, HD:2 * HD], ALU.add),
             "nrmb", ("nrm", "gnrep"))
        for ot in range(2):
            for c8 in range(8):
                P.mm(pb[ot][:, :], dd5t[:, c8 * TS + cc * C: c8 * TS + (cc + 1) * C],
                     gate[:, c8 * E + ot * 512: c8 * E + (ot + 1) * 512],
                     f"pb{ot}", (f"dd5t{c8}", "gate"), start=(c8 == 0), stop=(c8 == 7))
            P.op("s", lambda e, o=gs[:, ot * 512:(ot + 1) * 512], i=pb[ot][:, :]:
                 e.activation(o, i, ACT.Silu), "gsil", (f"pb{ot}",))
        P.op("v", lambda e: e.tensor_tensor(pr[:, :], nrb[:, :], gs[:, :], ALU.mult),
             f"prod{p}", ("nrmb", "gsil"))

    def chunk_back(tau, cc):
        p = cc % 2
        pr = prod[p]
        for c8 in range(8):
            bi = 4 + (c8 % 2)
            P.op("t", lambda e, o=pb[bi][:, 0:128], i=pr[:, c8 * 128:(c8 + 1) * 128]:
                 e.transpose(o, i, B["idnF"][:, :]), f"pb{bi}", (f"prod{p}", "idnF"))
            P.op("s", lambda e, o=prodT[:, c8 * TS + cc * C: c8 * TS + (cc + 1) * C],
                 i=pb[bi][:, 0:128]: e.copy(o, i), f"prodT{c8}", (f"pb{bi}",))

    ld3(wfc, io["WfcT"], "wfc", FO)
    for tau in range(NTT):
        t0 = tau * TS
        if tau > 0:
            for c8 in range(8):
                P.dma("gq", xt[:, c8 * TS:(c8 + 1) * TS],
                      io["xT"][c8 * 128:(c8 + 1) * 128, t0 + 1:t0 + TS + 1], f"xt{c8}")
                P.dma("q", dd5t[:, c8 * TS:(c8 + 1) * TS],
                      io["dd5H"][c8 * 128:(c8 + 1) * 128, t0:t0 + TS], f"dd5t{c8}")
                P.dma("q", rfmt[:, c8 * TS:(c8 + 1) * TS],
                      io["rfmH"][c8 * 128:(c8 + 1) * 128, t0:t0 + TS], f"rfmt{c8}")
        chunk_front(tau, 0)
        for cc in range(CPT):
            if cc + 1 < CPT:
                chunk_front(tau, cc + 1)
            chunk_back(tau, cc)
        for o8 in range(8):
            bi = o8 % 2
            for c8 in range(8):
                P.mm(pb[bi][:, 0:TS], outw[:, c8 * E + o8 * 128: c8 * E + (o8 + 1) * 128],
                     prodT[:, c8 * TS:(c8 + 1) * TS], f"pb{bi}",
                     (f"prodT{c8}", "outw"), start=(c8 == 0), stop=(c8 == 7))
            P.op("v", lambda e, o=vres[:, o8 * TS:(o8 + 1) * TS], i=pb[bi][:, 0:TS],
                 b=outb[:, o8:o8 + 1], x=xt[:, o8 * TS:(o8 + 1) * TS]:
                 e.scalar_tensor_tensor(o, i, b, x, ALU.add, ALU.add),
                 f"vres{o8}", (f"pb{bi}", "outb", f"xt{o8}"))
        v2f = prodT
        for c8 in range(8):
            P.op("s", lambda e, o=v2f[:, c8 * TS:(c8 + 1) * TS],
                 i=vres[:, c8 * TS:(c8 + 1) * TS]: e.activation(o, i, ACT.Square),
                 f"prodT{c8}", (f"vres{c8}",))
        for c8 in range(8):
            P.mm(pst[0:1, 0:TS], onesc[:, 0:1], vres[:, c8 * TS:(c8 + 1) * TS],
                 "pb4", ["onesc"] + [f"vres{i}" for i in range(8)],
                 start=(c8 == 0), stop=(c8 == 7))
        for c8 in range(8):
            P.mm(pst2[0:1, 0:TS], onesc[:, 0:1], v2f[:, c8 * TS:(c8 + 1) * TS],
                 "pb5", ["onesc"] + [f"prodT{i}" for i in range(8)],
                 start=(c8 == 0), stop=(c8 == 7))
        ln_rows(P, pst, pst2, rowst[0:1, 0:TS], sigI[0:1, 0:TS],
                "lm1", "sigI", TS, LN_EPS)
        for mb, (rsl, rk) in enumerate(((rowst[0:1, 0:TS], "lm1"),
                                        (sigI[0:1, 0:TS], "sigI"))):
            P.mm(pb[2 + mb][:, 0:TS], B["onesrF"][0:1, 0:128], rsl,
                 f"pb{2 + mb}", (rk, "onesrF"))
        for c8 in range(8):
            P.op("v", lambda e, o=vres[:, c8 * TS:(c8 + 1) * TS],
                 a=vres[:, c8 * TS:(c8 + 1) * TS], b=pb[2][:, 0:TS]:
                 e.tensor_tensor(o, a, b, ALU.subtract), f"vres{c8}", (f"vres{c8}", "pb2"))
            P.op("v", lambda e, o=out1[:, c8 * T + t0: c8 * T + t0 + TS],
                 a=vres[:, c8 * TS:(c8 + 1) * TS], b=pb[3][:, 0:TS]:
                 e.tensor_tensor(o, a, b, ALU.mult), f"out1_{tau}_{c8}",
                 (f"vres{c8}", "pb3"))



def emit_p2_ffn(P, nc, io, B, T, NTT, TS, FT, NFT, FO):
    wfc, wpj = B["wfc"], B["wpj"]
    onesc, onesr = B["onesc"], B["onesr"]
    out1, hbuf = B["out1"], B["hbuf"]
    sacc, vres2 = B["sacc"], B["vres2"]
    rowst2 = B["rowst2"]
    relu = B["relu"]
    pb = [B[f"pb{i}"] for i in range(6)]
    pst, pst2 = B["pb4"], B["pb5"]

    def ld3(sbuf, hbm, wkey, csz):
        P.dma("gq", sbuf[:, :].rearrange("p (a c) -> p a c", c=csz),
              hbm.rearrange("(a p) c -> p a c", p=128), wkey)

    ld3(wpj, io["WprojT"], "wpj", E)

    def tail(ft):
        t0 = ft * FT
        v22f = hbuf[:, 24 * FT:32 * FT]
        hk8 = [f"h{24 + i}" for i in range(8)]
        for c8 in range(8):
            sl2 = slice(c8 * FT, (c8 + 1) * FT)
            P.op("s", lambda e, o=v22f[:, sl2], i=vres2[:, sl2]:
                 e.activation(o, i, ACT.Square), f"h{24 + c8}", (f"vr2_{c8}",))
        yield
        for c8 in range(8):
            sl2 = slice(c8 * FT, (c8 + 1) * FT)
            P.mm(pst[0:1, 0:FT], onesc[:, 0:1], vres2[:, sl2],
                 "pb4", ["onesc"] + [f"vr2_{i}" for i in range(8)],
                 start=(c8 == 0), stop=(c8 == 7))
        yield
        for c8 in range(8):
            sl2 = slice(c8 * FT, (c8 + 1) * FT)
            P.mm(pst2[0:1, 0:FT], onesc[:, 0:1], v22f[:, sl2],
                 "pb5", ["onesc"] + hk8, start=(c8 == 0), stop=(c8 == 7))
        yield
        ln_rows(P, pst, pst2, rowst2[0:1, 0:FT], relu[0:1, 0:FT],
                "lm2", "relu", FT, LN_EPS)
        for mb, (rsl, rk2) in enumerate(((rowst2[0:1, 0:FT], "lm2"),
                                         (relu[0:1, 0:FT], "relu"))):
            bk = (3, 5)[mb]
            P.mm(pb[bk][:, 0:FT], B["onesrF"][0:1, 0:128], rsl,
                 f"pb{bk}", (rk2, "onesrF"))
        yield
        for c8 in range(8):
            sl = slice(c8 * FT, (c8 + 1) * FT)
            osl = rowst2[:, 0:FT]
            P.op("v", lambda e, o=vres2[:, sl], a=vres2[:, sl], b=pb[3][:, 0:FT]:
                 e.tensor_tensor(o, a, b, ALU.subtract), f"vr2_{c8}", (f"vr2_{c8}", "pb3"))
            P.op("v", lambda e, o=osl, a=vres2[:, sl], b=pb[5][:, 0:FT]:
                 e.tensor_tensor(o, a, b, ALU.mult), "lm2", (f"vr2_{c8}", "pb5"))
            P.dma("q", io["o_out"][c8 * 128:(c8 + 1) * 128, t0:t0 + FT], osl,
                  f"oo_{ft}_{c8}", ("lm2",), grp="qoo")
            if c8 % 2 == 1:
                yield

    tgen = None
    for ft in range(NFT):
        t0 = ft * FT
        o1keys = [f"out1_{t0 // TS}_{c8}" for c8 in range(8)]
        for f8 in range(32):
            bi = f8 % 3
            for c8 in range(8):
                P.mm(pb[bi][:, 0:FT], wfc[:, c8 * FO + f8 * 128: c8 * FO + (f8 + 1) * 128],
                     out1[:, c8 * T + t0: c8 * T + t0 + FT], f"pb{bi}",
                     ["wfc"] + o1keys, start=(c8 == 0), stop=(c8 == 7))
            P.op("s", lambda e, o=relu[:, 0:FT], i=pb[bi][:, 0:FT]:
                 e.activation(o, i, ACT.Relu), "relu", (f"pb{bi}",))
            P.op("s", lambda e, o=hbuf[:, f8 * FT:(f8 + 1) * FT], i=relu[:, 0:FT]:
                 e.activation(o, i, ACT.Square), f"h{f8}", ("relu",))
            if tgen is not None and f8 % 3 == 2:
                next(tgen, None)
        if tgen is not None:
            for _ in tgen:
                pass
        hkeys = [f"h{i}" for i in range(32)]
        for o8 in range(8):
            bi = 3 + o8 % 2
            for f8 in range(32):
                P.mm(pb[bi][:, 0:FT], wpj[:, f8 * E + o8 * 128: f8 * E + (o8 + 1) * 128],
                     hbuf[:, f8 * FT:(f8 + 1) * FT], f"pb{bi}",
                     ["wpj"] + hkeys, start=(f8 == 0), stop=(f8 == 31))
            sl = slice(o8 * FT, (o8 + 1) * FT)
            P.op("v", lambda e, o=sacc[:, 0:FT], i=pb[bi][:, 0:FT]:
                 e.tensor_single_scalar(o, i, THRESH[0], ALU.is_gt),
                 "sacc", (f"pb{bi}",))
            for jt in range(1, 5):
                P.op("v",
                     lambda e, o=sacc[:, 0:FT], i=pb[bi][:, 0:FT], s_=sacc[:, 0:FT], th=THRESH[jt]:
                     e.scalar_tensor_tensor(o, i, th, s_, ALU.is_gt, ALU.add),
                     "sacc", (f"pb{bi}", "sacc"))
            P.op("v", lambda e, o=vres2[:, sl], i=sacc[:, 0:FT],
                 b=out1[:, o8 * T + t0: o8 * T + t0 + FT]:
                 e.scalar_tensor_tensor(o, i, 0.2, b, ALU.mult, ALU.add),
                 f"vr2_{o8}", tuple(["sacc"] + o1keys))
        tgen = tail(ft)
    for _ in tgen:
        pass

    P.finish()


def ln_rows(P, pst, pst2, a, b, akey, bkey, W, eps):
    t = pst[0:1, 0:W]
    P.op("v", lambda e: e.tensor_scalar_mul(a, t, 1.0 / E), akey, ("pb4",))
    P.op("v", lambda e: e.tensor_scalar_mul(b, pst2[0:1, 0:W], 1.0 / E), bkey, ("pb5",))
    P.op("v", lambda e: e.scalar_tensor_tensor(t, a, 1.0, a, ALU.mult, ALU.mult),
         "pb4", (akey,))
    P.op("v", lambda e: e.tensor_tensor(b, b, t, ALU.subtract), bkey, (bkey, "pb4"))
    P.op("v", lambda e: e.tensor_scalar_add(b, b, eps), bkey, (bkey,))
    P.op("s", lambda e: e.activation(b, b, ACT.Sqrt), bkey, (bkey,))
    P.op("v", lambda e: e.reciprocal(b, b), bkey, (bkey,))


def prep_shared(inputs):
    lam = np.asarray(inputs["L0_attn_lora_lambda"], np.float32)
    Aw = np.asarray(inputs["L0_attn_lora_A_w"], np.float32)
    Ab = np.asarray(inputs["L0_attn_lora_A_b"], np.float32)
    Bw = np.asarray(inputs["L0_attn_lora_B_w"], np.float32)
    Bb = np.asarray(inputs["L0_attn_lora_B_b"], np.float32)
    mhd = np.asarray(inputs["L0_attn_mhd_w"], np.float32)
    wkvA = np.asarray(inputs["L0_attn_wkv_A"], np.float32)
    wkvB = np.asarray(inputs["L0_attn_wkv_B"], np.float32)
    wlam = np.asarray(inputs["L0_attn_wkv_lamb"], np.float32)

    laA = np.zeros((E, 3 * 128), np.float32)
    laAb = np.zeros((128, 3), np.float32)
    for p in range(3):
        laA[:, p * 128 + 0:p * 128 + 32] = Aw[2 * p]
        laA[:, p * 128 + 64:p * 128 + 96] = Aw[2 * p + 1]
        laAb[0:32, p] = Ab[2 * p]
        laAb[64:96, p] = Ab[2 * p + 1]
    laB = np.zeros((3 * 128, E), np.float32)
    for p in range(3):
        for half in range(2):
            g = 2 * p + half
            r0 = p * 128 + 64 * half
            laB[r0:r0 + 32] = Bw[g]
            laB[r0 + 32] = lam[g]
            laB[r0 + 33] = Bb[g]
    mhdBD = np.zeros((5 * 8 * 128, 128), np.float32)
    for g in range(5):
        for p in range(8):
            blk = mhdBD[(g * 8 + p) * 128:(g * 8 + p + 1) * 128]
            blk[0:64, 0:64] = mhd[g, 2 * p]
            blk[64:128, 64:128] = mhd[g, 2 * p + 1]
    wkvABD = np.zeros((8 * 128, 128), np.float32)
    wkvBBD = np.zeros((8 * 128, 128), np.float32)
    for p in range(8):
        wkvABD[p * 128:p * 128 + 64, 0:64] = wkvA[2 * p]
        wkvABD[p * 128 + 64:(p + 1) * 128, 64:128] = wkvA[2 * p + 1]
        wkvBBD[p * 128:p * 128 + 64, 0:64] = wkvB[2 * p]
        wkvBBD[p * 128 + 64:(p + 1) * 128, 64:128] = wkvB[2 * p + 1]
    lambP = np.zeros((1, 8 * 128), np.float32)
    for p in range(8):
        lambP[0, p * 128:p * 128 + 64] = wlam[2 * p]
        lambP[0, p * 128 + 64:(p + 1) * 128] = wlam[2 * p + 1]
    onesC = np.zeros((128, 4), np.float32)
    onesC[:, 0] = 1.0
    onesC[0:64, 1] = 1.0
    onesC[64:128, 2] = 1.0
    onesR = np.zeros((1, 384), np.float32)
    onesR[0, 0:128] = 1.0
    onesR[0, 128:192] = 1.0
    onesR[0, 320:384] = 1.0
    muX = np.ascontiguousarray(
        np.asarray(inputs["L0_attn_mu_x"], np.float32).reshape(8, 128).T)
    gnGB = np.concatenate([np.asarray(inputs["L0_attn_gn_gamma"], np.float32),
                           np.asarray(inputs["L0_attn_gn_beta"], np.float32)])[None, :]
    outB = np.ascontiguousarray(
        np.asarray(inputs["L0_attn_out_b"], np.float32).reshape(8, 128).T)
    return dict(
        laA=laA, laAb=laAb, laB=laB, mhdBD=mhdBD, wkvABD=wkvABD, wkvBBD=wkvBBD,
        lambP=lambP, CLmats=make_matrices(), masksC=make_masks(), muX=muX,
        onesC=onesC, onesR=onesR, idnC=np.eye(128, dtype=np.float32),
        gateW=np.ascontiguousarray(np.asarray(inputs["L0_attn_gate_w"], np.float32)),
        outW=np.ascontiguousarray(np.asarray(inputs["L0_attn_out_w"], np.float32)),
        outB=outB, gnGB=gnGB,
        WfcT=np.ascontiguousarray(np.asarray(inputs["L0_ffn_W_fc"], np.float32).T),
        WprojT=np.ascontiguousarray(np.asarray(inputs["L0_ffn_W_proj"], np.float32).T),
    )


P1_KEYS = ["laA", "laAb", "laB", "mhdBD", "wkvABD", "wkvBBD", "lambP",
           "CLmats", "masksC", "muX", "onesC", "onesR", "idnC"]
P2_KEYS = ["gateW", "outW", "outB", "gnGB", "WfcT", "WprojT", "onesC", "onesR", "idnC"]


def kernel_2launch(**inputs):
    x = np.asarray(inputs["L0_attn_x"], np.float32)[0]
    Tfull = x.shape[0]
    Tc = Tfull // NCORES
    shared = prep_shared(inputs)
    xT = np.ascontiguousarray(x.T)
    xTp = np.concatenate([np.zeros((E, 1), np.float32), xT], 1)

    nc1 = build_p1(Tc)
    in1 = []
    for cid in range(NCORES):
        m = {k: shared[k] for k in P1_KEYS}
        m["xT"] = np.ascontiguousarray(xTp[:, cid * Tc: cid * Tc + Tc + 1])
        in1.append(m)
    res1 = run_bass_kernel_spmd(nc1, in1, core_ids=list(range(NCORES)))
    r1 = res1.results

    nc2 = build_p2(Tc)
    in2 = []
    zsig = np.zeros((128, 512), np.float32)
    ones_qt = np.ones((HD,), np.float32)
    for cid in range(NCORES):
        m = {k: shared[k] for k in P2_KEYS}
        m["xT"] = in1[cid]["xT"]
        m["rwkv"] = r1[cid]["o_rwkv"]
        m["qglob"] = r1[cid]["o_qglob"]
        m["rfmH"] = r1[cid]["o_rfm"]
        m["dd5H"] = r1[cid]["o_dd5"]
        sigs, qts = [], []
        for j in range(7):
            if j < cid:
                sigs.append(np.asarray(r1[j]["o_sig"], np.float32))
                qts.append(np.asarray(r1[j]["o_qtot"], np.float32).reshape(HD))
            else:
                sigs.append(zsig)
                qts.append(ones_qt)
        m["sigAll"] = np.ascontiguousarray(np.concatenate(sigs, 0))
        m["qtAll"] = np.ascontiguousarray(np.stack(qts, 0))
        in2.append(m)
    res2 = run_bass_kernel_spmd(nc2, in2, core_ids=list(range(NCORES)))
    outs = [np.asarray(r["o_out"], np.float32).T for r in res2.results]
    return np.ascontiguousarray(np.concatenate(outs, 0))[None, :, :]


def build_fused(T):
    NT = max(1, T // 512)
    TS = T // NT
    CPT = TS // C
    NCH = T // C
    NTT = NT
    FT = min(T, 512)
    NFT = T // FT
    FO = 4096
    nc = bass.Bass("TRN2", target_bir_lowering=False, debug=False)
    dt = nc.dram_tensor
    io = {}
    io["xT"] = dt("xT", [E, T + 1], F32, kind="ExternalInput").ap()
    io["laA"] = dt("laA", [E, 3 * 128], F32, kind="ExternalInput").ap()
    io["laAb"] = dt("laAb", [128, 3], F32, kind="ExternalInput").ap()
    io["laB"] = dt("laB", [3 * 128, E], F32, kind="ExternalInput").ap()
    io["mhdBD"] = dt("mhdBD", [5 * 8 * 128, 128], F32, kind="ExternalInput").ap()
    io["wkvABD"] = dt("wkvABD", [8 * 128, 128], F32, kind="ExternalInput").ap()
    io["wkvBBD"] = dt("wkvBBD", [8 * 128, 128], F32, kind="ExternalInput").ap()
    io["lambP"] = dt("lambP", [1, 8 * 128], F32, kind="ExternalInput").ap()
    io["CLmats"] = dt("CLmats", [8 * 128, 128], F32, kind="ExternalInput").ap()
    io["masksC"] = dt("masksC", [3 * 128, 128], F32, kind="ExternalInput").ap()
    io["muX"] = dt("muX", [128, 8], F32, kind="ExternalInput").ap()
    io["onesC"] = dt("onesC", [128, 4], F32, kind="ExternalInput").ap()
    io["onesR"] = dt("onesR", [1, 384], F32, kind="ExternalInput").ap()
    io["idnC"] = dt("idnC", [128, 128], F32, kind="ExternalInput").ap()
    io["gateW"] = dt("gateW", [E, E], F32, kind="ExternalInput").ap()
    io["outW"] = dt("outW", [E, E], F32, kind="ExternalInput").ap()
    io["outB"] = dt("outB", [128, 8], F32, kind="ExternalInput").ap()
    io["gnGB"] = dt("gnGB", [1, 2 * HD], F32, kind="ExternalInput").ap()
    io["WfcT"] = dt("WfcT", [E, FO], F32, kind="ExternalInput").ap()
    io["WprojT"] = dt("WprojT", [FO, E], F32, kind="ExternalInput").ap()
    io["cmaskC"] = dt("cmaskC", [128, 16], F32, kind="ExternalInput").ap()
    io["o_out"] = dt("o_out", [E, T], F32, kind="ExternalOutput").ap()
    rwkvI = dt("rwkvI", [T, HD], BF16, kind="Internal").ap()
    qglobI = dt("qglobI", [T, HD], BF16, kind="Internal").ap()
    rfmI = dt("rfmI", [E, T], BF16, kind="Internal").ap()
    dd5I = dt("dd5I", [E, T], BF16, kind="Internal").ap()
    ccsig = dt("ccsig", [128, 512], F32, kind="Internal").ap()
    ccqt = dt("ccqt", [1, HD], F32, kind="Internal").ap()
    ccsigG = dt("ccsigG", [NCORES * 128, 512], F32, kind="Internal", addr_space="Shared").ap()
    ccqtG = dt("ccqtG", [NCORES, HD], F32, kind="Internal", addr_space="Shared").ap()

    io1 = dict(io)
    io1.update({"o_rwkv": rwkvI, "o_qglob": qglobI, "o_rfm": rfmI, "o_dd5": dd5I,
                "o_sig": ccsig, "o_qtot": ccqt, "fused": True})
    io2 = dict(io)
    io2.update({"rwkv": rwkvI, "qglob": qglobI, "rfmH": rfmI, "dd5H": dd5I,
                "sigAll": ccsigG, "qtAll": ccqtG, "fused": True})

    sems = ExitStack()
    with sems, nc.Block() as blk:
        P = Prog(nc, sems)

        @blk.sync
        def _(_q):
            s1 = ExitStack()
            with s1:
                sb = lambda n_, s_, d_: s1.enter_context(nc.sbuf_tensor(n_, s_, d_))
                psm = lambda n_, s_, d_=F32: s1.enter_context(nc.psum_tensor(n_, s_, d_))
                B = {}
                B["laA"] = sb("s_laA", [128, 8 * 384], BF16)
                B["laAb"] = sb("s_laAb", [128, 3], F32)
                for p in range(3):
                    B[f"laB{p}"] = sb(f"laB{p}", [128, E], BF16)
                B["mhd"] = sb("s_mhd", [128, 40 * 128], BF16)
                B["wkvA"] = sb("s_wkvA", [128, 8 * 128], BF16)
                B["wkvB"] = sb("s_wkvB", [128, 8 * 128], BF16)
                B["lamb"] = sb("s_lamb", [1, 8 * 128], BF16)
                B["mats"] = sb("s_mats", [128, 8 * 128], BF16)
                B["masks"] = sb("s_masks", [128, 3 * 128], BF16)
                B["mu"] = sb("s_mu", [128, 8], F32)
                B["onesc"] = sb("s_onesc", [128, 4], BF16)
                B["onesr"] = sb("s_onesr", [1, 384], BF16)
                B["idn"] = sb("s_idn", [128, 128], BF16)
                B["xt"] = sb("s_xt", [128, 8 * (TS + 1)], F32)
                B["diff"] = sb("s_diff", [128, 8 * TS], BF16)
                B["scr"] = sb("s_scr", [128, 8 * TS], BF16)
                B["zb"] = sb("s_zb", [128, 8 * TS], BF16)
                for p in range(3):
                    B[f"hdn{p}"] = sb(f"hdn{p}", [128, TS], BF16)
                for p in range(2):
                    B[f"ddg{p}"] = sb(f"ddg{p}", [128, 8 * TS], BF16)
                B["vfm"] = sb("s_vfm", [128, 8 * TS], BF16)
                B["rfm"] = sb("s_rfm", [128, 8 * TS], BF16)
                B["ktm"] = sb("s_ktm", [128, CPT * HD], BF16)
                B["vtm"] = sb("s_vtm", [128, CPT * HD], BF16)
                B["utm"] = sb("s_utm", [128, CPT * HD], BF16)
                B["rhotm"] = sb("s_rhotm", [128, CPT * HD], BF16)
                B["fac"] = sb("s_fac", [128, 8 * HD], BF16)
                B["Apl"] = sb("s_Apl", [128, 16 * 128], BF16)
                B["Amx"] = sb("s_Amx", [128, 16 * 128], BF16)
                B["kqb"] = sb("s_kqb", [128, 6 * HD], BF16)
                B["tmpb"] = sb("s_tmpb", [128, HD], BF16)
                B["ukb"] = sb("s_ukb", [128, HD], BF16)
                B["vrprod"] = sb("s_vrprod", [128, 8 * C], BF16)
                B["vrsb"] = sb("s_vrsb", [128, 16], BF16)
                B["sig"] = sb("s_sig", [128, 512], F32)
                B["sigsc"] = sb("s_sigsc", [128, 512], F32)
                B["sigb"] = sb("s_sigb", [128, 512], BF16)
                B["qcrow"] = sb("s_qcrow", [1, HD], BF16)
                B["rowsF"] = sb("s_rowsF", [128, 2 * HD], F32)
                B["onesrF"] = sb("s_onesrF", [1, 128], F32)
                B["rwkvo"] = sb("s_rwkvo", [128, 2 * HD], BF16)
                B["qglo"] = sb("s_qglo", [128, HD], BF16)
                for i in range(4):
                    B[f"pb{i}"] = psm(f"pb{i}", [128, 512])
                B["pxl"] = psm("s_pxl", [128, 1024])
                B["pacc"] = psm("s_pacc", [128, 1024])
                emit_p1(P, nc, io1, B, T, NT, TS, CPT, NCH)
            P.barrier()

            def _collectives():
                P.op("g", lambda e: e.collective_compute(
                    "AllGather", ALU.bypass, replica_groups=[list(range(NCORES))],
                    ins=[ccsig[:, :]], outs=[ccsigG[:, :]]), "ccsigG", ("osig",))
                P.op("g", lambda e: e.collective_compute(
                    "AllGather", ALU.bypass, replica_groups=[list(range(NCORES))],
                    ins=[ccqt[:, :]], outs=[ccqtG[:, :]]), "ccqtG", ("oqt",))
            io2["collectives"] = _collectives
            s2 = ExitStack()
            with s2:
                sb2 = lambda n_, s_, d_: s2.enter_context(nc.sbuf_tensor(n_, s_, d_))
                ps2 = lambda n_, s_, d_=F32: s2.enter_context(nc.psum_tensor(n_, s_, d_))
                B2 = {}
                B2["wfc"] = sb2("f_wfc", [128, 8 * FO], BF16)
                B2["onesc"] = sb2("f_onesc", [128, 4], BF16)
                B2["onesr"] = sb2("f_onesr", [1, 384], BF16)
                B2["onesrF"] = sb2("f_onesrF", [1, 128], F32)
                B2["out1"] = sb2("f_out1", [128, 8 * T], BF16)
                for i in range(6):
                    B2[f"pb{i}"] = ps2(f"f_pb{i}", [128, 512])
                B2["pst"] = ps2("f_pst", [1, 512])
                B2["pst2"] = ps2("f_pst2", [1, 512])
                a2 = ExitStack()
                with a2:
                    sba = lambda n_, s_, d_: a2.enter_context(nc.sbuf_tensor(n_, s_, d_))
                    B2["cmask"] = sba("f_cmask", [128, 16], F32)
                    B2["gate"] = sba("f_gate", [128, 8 * E], BF16)
                    B2["outw"] = sba("f_outw", [128, 8 * E], BF16)
                    B2["outb"] = sba("f_outb", [128, 8], F32)
                    B2["gnrep"] = sba("f_gnrep", [128, 2 * HD], BF16)
                    B2["sigI"] = sba("f_sigI", [128, 512], F32)
                    B2["sigB"] = sba("f_sigB", [128, 512], BF16)
                    B2["xt"] = sba("f_xt", [128, 8 * TS], BF16)
                    B2["rwkvc"] = sba("f_rwkvc", [128, HD], F32)
                    B2["rwkvb"] = [sba("f_rwkvb" + str(i2), [128, HD], BF16)
                                      for i2 in range(2)]
                    B2["qgl"] = [sba("f_qgl" + str(i2), [128, HD], BF16)
                                      for i2 in range(2)]
                    B2["stats"] = sba("f_stats", [128, 128], F32)
                    B2["nrm"] = sba("f_nrm", [128, HD], F32)
                    B2["nrmb"] = sba("f_nrmb", [128, HD], BF16)
                    B2["gsil"] = sba("f_gsil", [128, HD], BF16)
                    B2["prod"] = [sba("f_prod" + str(i2), [128, HD], F32)
                                      for i2 in range(2)]
                    B2["idnF"] = sba("f_idnF", [128, 128], F32)
                    B2["idnB"] = sba("f_idnB", [128, 128], BF16)
                    B2["prodT"] = sba("f_prodT", [128, 8 * TS], BF16)
                    B2["dd5t"] = sba("f_dd5t", [128, 8 * TS], BF16)
                    B2["rfmt"] = sba("f_rfmt", [128, 8 * TS], BF16)
                    B2["vres"] = sba("f_vres", [128, 8 * TS], BF16)
                    B2["rowst"] = sba("f_rowst", [128, TS], F32)
                    P.dma("q", B2["cmask"][:, :], io["cmaskC"][:, :], "cmask")
                    emit_p2_attn(P, nc, io2, B2, T, NTT, TS, CPT, FO)
                P.barrier()
                f2 = ExitStack()
                with f2:
                    sbf = lambda n_, s_, d_: f2.enter_context(nc.sbuf_tensor(n_, s_, d_))
                    B2["wpj"] = sbf("f_wpj", [128, 32 * E], BF16)
                    B2["hbuf"] = sbf("f_hbuf", [128, 32 * FT], BF16)
                    B2["relu"] = sbf("f_relu", [128, FT], F32)
                    B2["sacc"] = sbf("f_sacc", [128, FT], BF16)
                    B2["vres2"] = sbf("f_vres2", [128, 8 * FT], BF16)
                    B2["rowst2"] = sbf("f_rowst2", [128, FT], F32)
                    emit_p2_ffn(P, nc, io2, B2, T, NTT, TS, FT, NFT, FO)
    return nc


def kernel(**inputs):
    x = np.asarray(inputs["L0_attn_x"], np.float32)[0]
    Tfull = x.shape[0]
    Tc = Tfull // NCORES
    shared = prep_shared(inputs)
    xT = np.ascontiguousarray(x.T)
    xTp = np.concatenate([np.zeros((E, 1), np.float32), xT], 1)

    nc = build_fused(Tc)
    in_maps = []
    for cid in range(NCORES):
        m = {k: shared[k] for k in P1_KEYS + P2_KEYS if k in shared}
        m["xT"] = np.ascontiguousarray(xTp[:, cid * Tc: cid * Tc + Tc + 1])
        cm = np.zeros((128, 16), np.float32)
        for j in range(8):
            cm[:, j] = 1.0 if j < cid else 0.0
            cm[:, 8 + j] = 0.0 if j < cid else 1.0
        m["cmaskC"] = cm
        in_maps.append(m)
    res = run_bass_kernel_spmd(nc, in_maps, core_ids=list(range(NCORES)))
    outs = [np.asarray(r["o_out"], np.float32).T for r in res.results]
    return np.ascontiguousarray(np.concatenate(outs, 0))[None, :, :]


# version 39
# speedup vs baseline: 1.0066x; 1.0066x over previous
from contextlib import ExitStack

import numpy as np
import ml_dtypes

import concourse.bass as bass
import concourse.mybir as mybir
from concourse.bass_utils import run_bass_kernel_spmd

F32 = mybir.dt.float32
BF16 = mybir.dt.bfloat16
ALU = mybir.AluOpType
ACT = mybir.ActivationFunctionType
AXL = mybir.AxisListType
bfdt = ml_dtypes.bfloat16

NCORES = 8
E = 1024
H = 16
D = 64
HD = H * D
C = 128
CLAMP = 76.0
GN_EPS = 0.001
LN_EPS = 1e-5
THRESH = [0.2441942809699397, 0.46396913384288535, 0.6861859295255305,
          0.9059607823984762, 1.083978413225562]
MATNAMES = ["CL128", "CL64", "CL32", "CL16", "SUF8F", "ASUF8", "PRE128", "SUF128"]


def make_matrices():
    j = np.arange(C)[:, None]
    p = np.arange(C)[None, :]

    def M_pre(B):
        return (((p // B) * B <= j) & (j < p)).astype(np.float32)

    def M_suf(B):
        return ((j >= p + 1) & (j < (p // B + 1) * B)).astype(np.float32)

    def M_asuf8():
        return -((j >= p) & (j < (p // 8 + 1) * 8)).astype(np.float32)

    def CL(B):
        return np.where((p % B) < (B // 2), M_suf(B // 2), M_pre(B // 2))

    mats = dict(CL128=CL(128), CL64=CL(64), CL32=CL(32), CL16=CL(16),
                SUF8F=M_suf(8), ASUF8=M_asuf8(), PRE128=M_pre(128), SUF128=M_suf(128))
    return np.concatenate([-mats[m] for m in MATNAMES], 0)


def make_masks():
    s = np.arange(C)[:, None]
    i = np.arange(C)[None, :]
    ML32 = ((s // 32 == i // 32) & (s % 32 < 16) & (i % 32 >= 16))
    ML16 = ((s // 16 == i // 16) & (s % 16 < 8) & (i % 16 >= 8))
    MD8 = ((s // 8 == i // 8) & (s < i))
    return np.concatenate([ML32, ML16, MD8], 0).astype(np.float32)


class Prog:
    def __init__(self, nc, stack):
        self.nc = nc
        self.stack = stack
        names = ("t", "v", "s", "g", "tn")
        self.sems = {e: stack.enter_context(nc.semaphore(f"sem_{e}")) for e in names}
        self.cnt = {e: 0 for e in names}
        self.waited = {}
        self.wr = {}
        self.rd = {}
        self._gr = []
        self.gsem = {}
        self.gcnt = {}

    def eng(self, e):
        n = self.nc
        return {"t": n.tensor, "v": n.vector, "s": n.scalar, "g": n.gpsimd,
                "tn": n.tensor, "q": n.sync, "gq": n.gpsimd}[e]

    def _wait(self, eng, peng, val):
        if isinstance(peng, tuple):
            grp = peng[1]
            val = self.gcnt[grp]
            key = (eng, peng)
            if self.waited.get(key, -1) >= val:
                return
            self.eng(eng).wait_ge(self.gsem[grp], val)
            self.waited[key] = val
            return
        key = (eng, peng)
        if self.waited.get(key, -1) >= val:
            return
        self.eng(eng).wait_ge(self.sems[peng], val)
        self.waited[key] = val

    def need(self, eng, keys):
        for k in keys:
            t = self.wr.get(k)
            if t is not None:
                self._wait(eng, t[0], t[1])

    def need_w(self, eng, keys):
        self.need(eng, keys)
        for k in keys:
            for peng, v in self.rd.get(k, {}).items():
                self._wait(eng, peng, v)

    def did(self, instr, eng, writes, reads):
        self.cnt[eng] += 1
        instr.then_inc(self.sems[eng], 1)
        v = self.cnt[eng]
        for k in writes:
            self.wr[k] = (eng, v)
            self.rd[k] = {}
        for k in reads:
            self.rd.setdefault(k, {})[eng] = v

    def dma(self, qe, out, in_, w, r=(), grp=None):
        if grp is None:
            grp = w if isinstance(w, str) else w[0]
            grp = qe + "".join(c for c in grp if c.isalpha())
        wt = w if isinstance(w, tuple) else (w,)
        self.need_w(qe, wt)
        self.need(qe, r)
        if grp not in self.gsem:
            self.gsem[grp] = self.stack.enter_context(self.nc.semaphore(f"gd_{grp}"))
            self.gcnt[grp] = 0
        i = self.eng(qe).dma_start(out=out, in_=in_)
        self.gcnt[grp] += 16
        i.then_inc(self.gsem[grp], 16)
        v = self.gcnt[grp]
        tok = ("dma", grp)
        for k in wt:
            self.wr[k] = (tok, v)
            self.rd[k] = {}
        for k in r:
            self.rd.setdefault(k, {})[tok] = v

    def mm(self, out, lhsT, rhs, w, r, start=True, stop=True, tile_position=None,
           skip_group_check=False, midsync=False):
        self.need("t", r)
        if start:
            self.need_w("t", (w,))
        kw = {}
        if tile_position is not None:
            kw["tile_position"] = tile_position
        if skip_group_check:
            kw["skip_group_check"] = True
        i = self.nc.tensor.matmul(out, lhsT, rhs, start=start, stop=stop, **kw)
        self._gr.extend(r)
        if stop:
            gr, self._gr = self._gr, []
            self.did(i, "t", (w,), gr)
        elif midsync:
            self.cnt["tn"] += 1
            self.nc.tensor.nop().then_inc(self.sems["tn"], 1)
            for k in r:
                self.rd.setdefault(k, {})["tn"] = self.cnt["tn"]


    def op(self, eng, fn, w, r):
        wt = w if isinstance(w, tuple) else (w,)
        self.need_w(eng, wt)
        self.need(eng, r)
        i = fn(self.eng(eng))
        self.did(i, eng, wt, r)

    def barrier(self):
        cur = dict(self.cnt)
        gcur = dict(self.gcnt)
        for e in ("t", "v", "s", "g"):
            for pe, v in cur.items():
                if v > 0 and pe != e:
                    self._wait(e, pe, v)
            for g, v in gcur.items():
                if v > 0:
                    self._wait(e, ("dma", g), v)
        for g, v in gcur.items():
            if v > 0:
                self._wait("q", ("dma", g), v)

    def finish(self, *qengs):
        for g, v in self.gcnt.items():
            self.eng("q").wait_ge(self.gsem[g], v)


def build_p1(T):
    NT = max(1, T // 512)
    TS = T // NT
    CPT = TS // C
    NCH = T // C
    nc = bass.Bass("TRN2", target_bir_lowering=False, debug=False)
    dt = nc.dram_tensor
    io = {}
    io["xT"] = dt("xT", [E, T + 1], F32, kind="ExternalInput").ap()
    io["laA"] = dt("laA", [E, 3 * 128], F32, kind="ExternalInput").ap()
    io["laAb"] = dt("laAb", [128, 3], F32, kind="ExternalInput").ap()
    io["laB"] = dt("laB", [3 * 128, E], F32, kind="ExternalInput").ap()
    io["mhdBD"] = dt("mhdBD", [5 * 8 * 128, 128], F32, kind="ExternalInput").ap()
    io["wkvABD"] = dt("wkvABD", [8 * 128, 128], F32, kind="ExternalInput").ap()
    io["wkvBBD"] = dt("wkvBBD", [8 * 128, 128], F32, kind="ExternalInput").ap()
    io["lambP"] = dt("lambP", [1, 8 * 128], F32, kind="ExternalInput").ap()
    io["CLmats"] = dt("CLmats", [8 * 128, 128], F32, kind="ExternalInput").ap()
    io["masksC"] = dt("masksC", [3 * 128, 128], F32, kind="ExternalInput").ap()
    io["muX"] = dt("muX", [128, 8], F32, kind="ExternalInput").ap()
    io["onesC"] = dt("onesC", [128, 4], F32, kind="ExternalInput").ap()
    io["onesR"] = dt("onesR", [1, 384], F32, kind="ExternalInput").ap()
    io["idnC"] = dt("idnC", [128, 128], F32, kind="ExternalInput").ap()
    io["o_rwkv"] = dt("o_rwkv", [T, HD], BF16, kind="ExternalOutput").ap()
    io["o_qglob"] = dt("o_qglob", [T, HD], BF16, kind="ExternalOutput").ap()
    io["o_rfm"] = dt("o_rfm", [E, T], BF16, kind="ExternalOutput").ap()
    io["o_dd5"] = dt("o_dd5", [E, T], BF16, kind="ExternalOutput").ap()
    io["o_sig"] = dt("o_sig", [128, 512], F32, kind="ExternalOutput").ap()
    io["o_qtot"] = dt("o_qtot", [1, HD], F32, kind="ExternalOutput").ap()

    st = ExitStack()
    with st:
        sb = lambda n_, s_, d_: st.enter_context(nc.sbuf_tensor(n_, s_, d_))
        psm = lambda n_, s_, d_=F32: st.enter_context(nc.psum_tensor(n_, s_, d_))
        B = {}
        B["laA"] = sb("s_laA", [128, 8 * 384], BF16)
        B["laAb"] = sb("s_laAb", [128, 3], F32)
        for p in range(3):
            B[f"laB{p}"] = sb(f"laB{p}", [128, E], BF16)
        B["mhd"] = sb("s_mhd", [128, 40 * 128], BF16)
        B["wkvA"] = sb("s_wkvA", [128, 8 * 128], BF16)
        B["wkvB"] = sb("s_wkvB", [128, 8 * 128], BF16)
        B["lamb"] = sb("s_lamb", [1, 8 * 128], BF16)
        B["mats"] = sb("s_mats", [128, 8 * 128], BF16)
        B["masks"] = sb("s_masks", [128, 3 * 128], BF16)
        B["mu"] = sb("s_mu", [128, 8], F32)
        B["onesc"] = sb("s_onesc", [128, 4], BF16)
        B["onesr"] = sb("s_onesr", [1, 384], BF16)
        B["idn"] = sb("s_idn", [128, 128], BF16)
        B["xt"] = sb("s_xt", [128, 8 * (TS + 1)], F32)
        B["diff"] = sb("s_diff", [128, 8 * TS], BF16)
        B["scr"] = sb("s_scr", [128, 8 * TS], BF16)
        B["zb"] = sb("s_zb", [128, 8 * TS], BF16)
        for p in range(3):
            B[f"hdn{p}"] = sb(f"hdn{p}", [128, TS], BF16)
        for p in range(2):
            B[f"ddg{p}"] = sb(f"ddg{p}", [128, 8 * TS], BF16)
        B["vfm"] = sb("s_vfm", [128, 8 * TS], BF16)
        B["rfm"] = sb("s_rfm", [128, 8 * TS], BF16)
        B["ktm"] = sb("s_ktm", [128, CPT * HD], BF16)
        B["vtm"] = sb("s_vtm", [128, CPT * HD], BF16)
        B["utm"] = sb("s_utm", [128, CPT * HD], BF16)
        B["rhotm"] = sb("s_rhotm", [128, CPT * HD], BF16)
        B["fac"] = sb("s_fac", [128, 8 * HD], BF16)
        B["Apl"] = sb("s_Apl", [128, 16 * 128], BF16)
        B["Amx"] = sb("s_Amx", [128, 16 * 128], BF16)
        B["kqb"] = sb("s_kqb", [128, 6 * HD], BF16)
        B["tmpb"] = sb("s_tmpb", [128, HD], BF16)
        B["ukb"] = sb("s_ukb", [128, HD], BF16)
        B["vrprod"] = sb("s_vrprod", [128, 8 * C], BF16)
        B["vrsb"] = sb("s_vrsb", [128, 16], BF16)
        B["sig"] = sb("s_sig", [128, 512], F32)
        B["sigsc"] = sb("s_sigsc", [128, 512], F32)
        B["sigb"] = sb("s_sigb", [128, 512], BF16)
        B["qcrow"] = sb("s_qcrow", [1, HD], BF16)
        B["rowsF"] = sb("s_rowsF", [128, 2 * HD], F32)
        B["onesrF"] = sb("s_onesrF", [1, 128], F32)
        B["rwkvo"] = sb("s_rwkvo", [128, 2 * HD], BF16)
        B["qglo"] = sb("s_qglo", [128, HD], BF16)
        for i in range(4):
            B[f"pb{i}"] = psm(f"pb{i}", [128, 512])
        B["pxl"] = psm("s_pxl", [128, 1024])
        B["pacc"] = psm("s_pacc", [128, 1024])

        sems = ExitStack()
        with sems, nc.Block() as blk:
            P = Prog(nc, sems)

            @blk.sync
            def _(_q):
                emit_p1(P, nc, io, B, T, NT, TS, CPT, NCH)
    return nc


def emit_p1(P, nc, io, B, T, NT, TS, CPT, NCH):
    laA, laAb = B["laA"], B["laAb"]
    laB = [B[f"laB{p}"] for p in range(3)]
    mhd, wkvA, wkvB, lamb = B["mhd"], B["wkvA"], B["wkvB"], B["lamb"]
    mats, masks, mu = B["mats"], B["masks"], B["mu"]
    onesc, onesr, idn = B["onesc"], B["onesr"], B["idn"]
    xt, diff, scr, zb = B["xt"], B["diff"], B["scr"], B["zb"]
    hdn = [B[f"hdn{p}"] for p in range(3)]
    ddg = [B["ddg0"], B["ddg1"]]
    vfm, rfm = B["vfm"], B["rfm"]
    whfm = B["scr"]
    wfm = B["zb"]
    ktm, vtm, utm, rhotm = B["ktm"], B["vtm"], B["utm"], B["rhotm"]
    fac, Apl = B["fac"], B["Apl"]
    Amx = B["Amx"]
    kqb, tmpb, ukb = B["kqb"], B["tmpb"], B["ukb"]
    vrprod, vrsb = B["vrprod"], B["vrsb"]
    sig, sigsc, sigb, qcrow = B["sig"], B["sigsc"], B["sigb"], B["qcrow"]
    rowsF = B["rowsF"]
    qtr = rowsF[0:1, 0:HD]
    rqr = rowsF[0:1, HD:2 * HD]
    onesrF, rwkvo, qglo = B["onesrF"], B["rwkvo"], B["qglo"]
    pb = [B[f"pb{i}"] for i in range(4)]
    pxl, pacc = B["pxl"], B["pacc"]

    def ld3(sbuf, hbm, wkey, csz=128):
        P.dma("gq", sbuf[:, :].rearrange("p (a c) -> p a c", c=csz),
              hbm.rearrange("(a p) c -> p a c", p=128), wkey)

    ld3(laA, io["laA"], "laA", 384)
    P.dma("q", laAb[:, :], io["laAb"][:, :], "laAb")
    for p in range(3):
        P.dma("gq", laB[p][:, :], io["laB"][p * 128:(p + 1) * 128, :], f"laB{p}")
    ld3(mhd, io["mhdBD"], "mhd")
    ld3(wkvA, io["wkvABD"], "wkvA")
    ld3(wkvB, io["wkvBBD"], "wkvB")
    P.dma("gq", lamb[:, :], io["lambP"][:, :], "lamb")
    ld3(mats, io["CLmats"], "mats")
    ld3(masks, io["masksC"], "masks")
    P.dma("q", mu[:, :], io["muX"][:, :], "mu")
    P.dma("gq", onesc[:, :], io["onesC"][:, :], "onesc")
    P.dma("gq", onesr[:, :], io["onesR"][:, :], "onesr")
    P.dma("q", onesrF[:, :], io["onesR"][0:1, 0:128], "onesrF")
    P.dma("gq", idn[:, :], io["idnC"][:, :], "idn")
    P.op("v", lambda e: e.memset(sig[:, :], 0.0), "sig", ())
    P.op("v", lambda e: e.memset(qtr, 1.0), "qtr", ())
    P.op("v", lambda e: e.memset(rqr, 1.0), "rqr", ())

    for tau in range(NT):
        t0 = tau * TS
        for c8 in range(8):
            P.dma("q", xt[:, c8 * (TS + 1):(c8 + 1) * (TS + 1)],
                  io["xT"][c8 * 128:(c8 + 1) * 128, t0:t0 + TS + 1], f"xt{c8}")
        for c8 in range(8):
            cur = xt[:, c8 * (TS + 1) + 1:(c8 + 1) * (TS + 1)]
            prv = xt[:, c8 * (TS + 1):(c8 + 1) * (TS + 1) - 1]
            df = diff[:, c8 * TS:(c8 + 1) * TS]
            sc = scr[:, c8 * TS:(c8 + 1) * TS]
            zc = zb[:, c8 * TS:(c8 + 1) * TS]
            P.op("v", lambda e, a=df, b=prv, c=cur: e.tensor_sub(a, b, c),
                 f"diff{c8}", (f"xt{c8}",))
            P.op("v", lambda e, a=sc, b=df, m=mu[:, c8:c8 + 1], x=cur:
                 e.scalar_tensor_tensor(a, b, m, x, ALU.mult, ALU.add),
                 f"scr{c8}", (f"diff{c8}", f"xt{c8}", "mu"))
            P.op("v", lambda e, a=sc, b=df: e.tensor_mul(a, b, a),
                 f"scr{c8}", (f"diff{c8}", f"scr{c8}"))
            P.op("v", lambda e, a=zc, b=sc, c=cur: e.tensor_add(a, b, c),
                 f"zb{c8}", (f"scr{c8}", f"xt{c8}"))
        for p in range(3):
            for c8 in range(8):
                P.mm(pb[p][:, 0:TS], laA[:, c8 * 384 + p * 128: c8 * 384 + (p + 1) * 128],
                     zb[:, c8 * TS:(c8 + 1) * TS], f"pb{p}", ("laA", f"zb{c8}"),
                     start=(c8 == 0), stop=(c8 == 7))
            P.op("s", lambda e, o=hdn[p][:, :], i=pb[p][:, 0:TS], b=laAb[:, p:p + 1]:
                 e.activation(o, i, ACT.Tanh, bias=b), f"hdn{p}", (f"pb{p}", "laAb"))
            P.op("v", lambda e, o=hdn[p][32:34, :]: e.memset(o, 1.0), f"hdn{p}", ())
            P.op("v", lambda e, o=hdn[p][96:98, :]: e.memset(o, 1.0), f"hdn{p}", ())
        for g in range(6):
            p, half = g // 2, g % 2
            rs = slice(64 * half, 64 * half + 34)
            dst = ddg[g % 2]
            dkey = f"ddg{g % 2}"
            for c8 in range(8):
                bi_ = c8 % 4
                bank, bkey = pb[bi_], f"pb{bi_}"
                P.mm(bank[:, 0:TS], laB[p][rs, c8 * 128:(c8 + 1) * 128],
                     hdn[p][rs, :], bkey, (f"laB{p}", f"hdn{p}"))
                sc = scr[:, c8 * TS:(c8 + 1) * TS]
                df = diff[:, c8 * TS:(c8 + 1) * TS]
                cur = xt[:, c8 * (TS + 1) + 1:(c8 + 1) * (TS + 1)]
                P.op("v", lambda e, a=sc, b=df, c=bank[:, 0:TS]: e.tensor_mul(a, b, c),
                     f"scr{c8}", (f"diff{c8}", bkey))
                P.op("g" if c8 % 2 == 0 else "v",
                     lambda e, a=dst[:, c8 * TS:(c8 + 1) * TS], b=sc, c=cur:
                     e.tensor_add(a, b, c), f"{dkey}_{c8}", (f"scr{c8}", f"xt{c8}"))
            if g == 5:
                for c8 in range(8):
                    P.dma("q", io["o_dd5"][c8 * 128:(c8 + 1) * 128, t0:t0 + TS],
                          dst[:, c8 * TS:(c8 + 1) * TS], f"odd5_{tau}_{c8}", (f"ddg1_{c8}",))
                continue
            gi = {0: "k", 1: "v", 2: "r", 3: "w", 4: "u"}[g]
            if gi in ("k", "v", "u"):
                dstt = {"k": ktm, "v": vtm, "u": utm}[gi]
                for pr in range(8):
                    for cc in range(CPT):
                        bi = (pr * CPT + cc) % 4
                        bank, bkey = pb[bi], f"pb{bi}"
                        P.mm(bank[:, 0:128],
                             dst[:, pr * TS + cc * C: pr * TS + (cc + 1) * C],
                             mhd[:, (g * 8 + pr) * 128:(g * 8 + pr + 1) * 128],
                             bkey, (f"{dkey}_{pr}", "mhd"))
                        osl = dstt[:, cc * HD + pr * 128: cc * HD + (pr + 1) * 128]
                        if (pr + cc) % 2 == 0:
                            P.op("v", lambda e, o=osl, i=bank[:, 0:128]: e.tensor_copy(o, i),
                                 f"{gi}tm{cc}_{pr}", (bkey,))
                        else:
                            P.op("s", lambda e, o=osl, i=bank[:, 0:128]: e.copy(o, i),
                                 f"{gi}tm{cc}_{pr}", (bkey,))
            if gi in ("v", "r", "w"):
                dstf = {"v": vfm, "r": rfm, "w": wfm}[gi]
                for pr in range(8):
                    bf_ = pr % 4
                    P.mm(pb[bf_][:, 0:TS], mhd[:, (g * 8 + pr) * 128:(g * 8 + pr + 1) * 128],
                         dst[:, pr * TS:(pr + 1) * TS], f"pb{bf_}", (f"{dkey}_{pr}", "mhd"))
                    dsl = dstf[:, pr * TS:(pr + 1) * TS]
                    if pr % 2 == 0:
                        P.op("v", lambda e, o=dsl, i=pb[bf_][:, 0:TS]: e.tensor_copy(o, i),
                             f"{gi}fm{pr}" if gi != "w" else f"zb{pr}", (f"pb{bf_}",))
                    else:
                        P.op("s", lambda e, o=dsl, i=pb[bf_][:, 0:TS]: e.copy(o, i),
                             f"{gi}fm{pr}" if gi != "w" else f"zb{pr}", (f"pb{bf_}",))
        for c8 in range(8):
            P.dma("q", io["o_rfm"][c8 * 128:(c8 + 1) * 128, t0:t0 + TS],
                  rfm[:, c8 * TS:(c8 + 1) * TS], f"orfm_{tau}_{c8}", (f"rfm{c8}",))
        for pr in range(8):
            bf_ = pr % 4
            P.mm(pb[bf_][:, 0:TS], wkvA[:, pr * 128:(pr + 1) * 128],
                 wfm[:, pr * TS:(pr + 1) * TS], f"pb{bf_}", (f"zb{pr}", "wkvA"))
            P.op("s", lambda e, o=whfm[:, pr * TS:(pr + 1) * TS], i=pb[bf_][:, 0:TS]:
                 e.activation(o, i, ACT.Tanh), f"scr{pr}", (f"pb{bf_}",))
        for pr in range(8):
            for cc in range(CPT):
                bi = (pr * CPT + cc) % 4
                bank, bkey = pb[bi], f"pb{bi}"
                P.mm(bank[:, 0:128], whfm[:, pr * TS + cc * C: pr * TS + (cc + 1) * C],
                     wkvB[:, pr * 128:(pr + 1) * 128], bkey,
                     (f"whfm{pr}", "wkvB"), start=True, stop=False)
                P.mm(bank[:, 0:128], onesr[0:1, 0:128], lamb[0:1, pr * 128:(pr + 1) * 128],
                     bkey, ("onesr", "lamb"), start=False, stop=True)
                P.op("s", lambda e, o=rhotm[:, cc * HD + pr * 128: cc * HD + (pr + 1) * 128],
                     i=bank[:, 0:128]: e.activation(o, i, ACT.Exp),
                     f"rhotm{cc}_{pr}", (bkey,))

        for cc in range(CPT):
            ch = tau * CPT + cc
            rkeys = [f"rhotm{cc}_{pr}" for pr in range(8)]
            for mpair in range(4):
                for mi in range(2):
                    m = mpair * 2 + mi
                    for hf in range(2):
                        bi = mi * 2 + hf
                        P.mm(pb[bi][:, :], mats[:, m * 128:(m + 1) * 128],
                             rhotm[:, cc * HD + hf * 512: cc * HD + (hf + 1) * 512],
                             f"pb{bi}", ["mats"] + rkeys)
                    if MATNAMES[m] == "ASUF8":
                        for hf in range(2):
                            bi = mi * 2 + hf
                            P.op("v", lambda e, o=pb[bi][:, :]:
                                 e.tensor_scalar_min(o, o, CLAMP), f"pb{bi}", (f"pb{bi}",))
                    for hf in range(2):
                        bi = mi * 2 + hf
                        P.op("s", lambda e, o=fac[:, m * HD + hf * 512: m * HD + (hf + 1) * 512],
                             i=pb[bi][:, :]: e.activation(o, i, ACT.Exp),
                             f"fac{m}", (f"pb{bi}",))
            for h in range(16):
                bi = h // 4
                rs = slice(64 * (h % 2), 64 * (h % 2) + 64)
                cols = slice((h // 2) * TS + cc * C, (h // 2) * TS + (cc + 1) * C)
                P.mm(pb[bi][:, (h % 4) * 128:(h % 4 + 1) * 128], vfm[rs, cols], rfm[rs, cols],
                     f"pb{bi}", (f"vfm{h // 2}", f"rfm{h // 2}"))
            for b in range(4):
                P.op("s", lambda e, o=Apl[:, b * 512:(b + 1) * 512], i=pb[b][:, :]:
                     e.copy(o, i), f"Apl{b}", (f"pb{b}",))
            aplk = ["Apl0", "Apl1", "Apl2", "Apl3"]
            ktmk = [f"ktm{cc}_{pr}" for pr in range(8)]
            for kqi, m in enumerate([0, 1, 2, 3, 4, 7]):
                P.op("v",
                     lambda e, o=kqb[:, kqi * HD:(kqi + 1) * HD],
                     a=ktm[:, cc * HD:(cc + 1) * HD], f=fac[:, m * HD:(m + 1) * HD]:
                     e.tensor_tensor(o, a, f, ALU.mult),
                     f"kq{kqi}", [f"fac{m}"] + ktmk)
            P.op("v", lambda e, o=ukb[:, :], a=utm[:, cc * HD:(cc + 1) * HD],
                 b=ktm[:, cc * HD:(cc + 1) * HD]: e.tensor_tensor(o, a, b, ALU.mult),
                 "ukb", [f"utm{cc}_{pr}" for pr in range(8)] + ktmk)
            v3o = vrprod[:, :].rearrange("p (a c) -> p a c", c=C)
            v3a = vfm[:, :].rearrange("p (a c) -> p a c", c=TS)[:, :, cc * C:(cc + 1) * C]
            v3b = rfm[:, :].rearrange("p (a c) -> p a c", c=TS)[:, :, cc * C:(cc + 1) * C]
            P.op("v", lambda e: e.tensor_tensor(v3o, v3a, v3b, ALU.mult), "vrprod",
                 [f"vfm{pr}" for pr in range(8)] + [f"rfm{pr}" for pr in range(8)])
            for pr in range(8):
                P.mm(pb[0][:, pr * 2:pr * 2 + 2], vrprod[:, pr * C:(pr + 1) * C],
                     onesc[:, 1:3], "pb0", ("vrprod", "onesc"))
            P.op("v", lambda e: e.tensor_copy(vrsb[:, :], pb[0][:, 0:16]), "vrsb", ("pb0",))
            t3o = tmpb[:, :].rearrange("p (h d) -> p h d", d=D)
            u3 = ukb[:, :].rearrange("p (h d) -> p h d", d=D)
            vr3 = vrsb[:, :].unsqueeze(2).broadcast_to([128, 16, D])
            P.op("v", lambda e: e.tensor_tensor(t3o, u3, vr3, ALU.mult), "tmpb", ("ukb", "vrsb"))
            P.mm(pacc[:, 0:512], idn[:, :], tmpb[:, 0:512], "pacc", ("idn", "tmpb"),
                 start=True, stop=False, skip_group_check=True)
            P.mm(pacc[:, 512:1024], idn[:, :], tmpb[:, 512:1024], "pacc", ("idn", "tmpb"),
                 start=True, stop=False, skip_group_check=True)

            def acc_scaled(lo, hi, xkey, fm):
                P.op("v", lambda e, o=tmpb[lo:hi, :], x=pxl[lo:hi, :],
                     f=fac[lo:hi, fm * HD:(fm + 1) * HD]:
                     e.tensor_tensor(o, x, f, ALU.mult), "tmpb", (xkey, f"fac{fm}"))
                tp = None if lo == 0 else (lo, lo)
                P.mm(pacc[lo:hi, 0:512], idn[lo:hi, lo:hi], tmpb[lo:hi, 0:512],
                     "pacc", ("idn", "tmpb"), start=False, stop=False,
                     tile_position=tp, skip_group_check=True)
                P.mm(pacc[lo:hi, 512:1024], idn[lo:hi, lo:hi], tmpb[lo:hi, 512:1024],
                     "pacc", ("idn", "tmpb"), start=False, stop=False,
                     tile_position=tp, skip_group_check=True)

            for h in range(16):
                P.mm(pxl[64:128, h * 64:(h + 1) * 64],
                     Apl[0:64, h * 128 + 64:(h + 1) * 128],
                     kqb[0:64, 0 * HD + h * 64: 0 * HD + (h + 1) * 64],
                     "pxl", (f"Apl{h // 4}", "kq0"), tile_position=(0, 64))
            acc_scaled(64, 128, "pxl", 0)
            for h in range(16):
                P.mm(pxl[32:64, h * 64:(h + 1) * 64],
                     Apl[0:32, h * 128 + 32: h * 128 + 64],
                     kqb[0:32, 1 * HD + h * 64: 1 * HD + (h + 1) * 64],
                     "pxl", (f"Apl{h // 4}", "kq1"), tile_position=(0, 32))
                P.mm(pxl[96:128, h * 64:(h + 1) * 64],
                     Apl[64:96, h * 128 + 96:(h + 1) * 128],
                     kqb[64:96, 1 * HD + h * 64: 1 * HD + (h + 1) * 64],
                     "pxl", (f"Apl{h // 4}", "kq1"), tile_position=(64, 96))
            acc_scaled(32, 64, "pxl", 1)
            acc_scaled(96, 128, "pxl", 1)
            for (ami, fm, kqi) in ((0, 2, 2), (1, 3, 3), (2, 5, 4)):
                a3 = Apl[:, :].rearrange("p (h c) -> p h c", c=128)
                m3 = masks[:, ami * 128:(ami + 1) * 128].unsqueeze(1).broadcast_to([128, 16, 128])
                o3 = Amx[:, :].rearrange("p (h c) -> p h c", c=128)
                P.op("v",
                     lambda e, o=o3, a=a3, m=m3: e.tensor_tensor(o, a, m, ALU.mult),
                     "Amx", aplk + ["masks"])
                for h in range(16):
                    P.mm(pxl[:, h * 64:(h + 1) * 64], Amx[:, h * 128:(h + 1) * 128],
                         kqb[:, kqi * HD + h * 64: kqi * HD + (h + 1) * 64],
                         "pxl", ("Amx", f"kq{kqi}"))
                acc_scaled(0, 128, "pxl", fm)
            P.op("v", lambda e: e.tensor_copy(sigb[:, :], sig[:, :]), "sigb", ("sig",))
            for h in range(16):
                rs = slice(64 * (h % 2), 64 * (h % 2) + 64)
                cols = slice((h // 2) * TS + cc * C, (h // 2) * TS + (cc + 1) * C)
                P.mm(pxl[:, h * 64:(h + 1) * 64], rfm[rs, cols],
                     sigb[rs, (h // 2) * 64:(h // 2) * 64 + 64],
                     "pxl", (f"rfm{h // 2}", "sigb"))
            P.op("v", lambda e, o=tmpb[:, :], x=pxl[:, :], f=fac[:, 6 * HD:7 * HD]:
                 e.tensor_tensor(o, x, f, ALU.mult), "tmpb", ("pxl", "fac6"))
            P.mm(pacc[:, 0:512], idn[:, :], tmpb[:, 0:512], "pacc", ("idn", "tmpb"),
                 start=False, stop=False, skip_group_check=True)
            P.mm(pacc[:, 512:1024], idn[:, :], tmpb[:, 512:1024], "pacc", ("idn", "tmpb"),
                 start=False, stop=True, skip_group_check=True)
            par = (ch % 2) * HD
            P.op("s", lambda e, o=rwkvo[:, par:par + HD],
                 i=pacc[:, 0:1024]: e.copy(o, i), f"rwkvo{ch % 2}", ("pacc",))
            P.dma("q", io["o_rwkv"][ch * 128:(ch + 1) * 128, :], rwkvo[:, par:par + HD],
                  f"orw{ch}", (f"rwkvo{ch % 2}",), grp="qorw")
            for hf in range(2):
                bi = 1 + hf
                P.mm(pb[bi][:, :], onesrF[0:1, 0:128],
                     rowsF[0:1, HD + hf * 512: HD + (hf + 1) * 512],
                     f"pb{bi}", ("onesrF", "rqr"))
                P.op("v", lambda e, o=qglo[:, hf * 512:(hf + 1) * 512],
                     f=fac[:, 6 * HD + hf * 512: 6 * HD + (hf + 1) * 512],
                     r=pb[bi][:, :]: e.tensor_tensor(o, f, r, ALU.mult),
                     "qglo", ("fac6", f"pb{bi}"))
            P.dma("q", io["o_qglob"][ch * 128:(ch + 1) * 128, :], qglo[:, :],
                  f"oqg{ch}", ("qglo",), grp="qoqg")
            for h in range(16):
                P.mm(pb[3][64 * (h % 2):64 * (h % 2) + 64, (h // 2) * 64:(h // 2) * 64 + 64],
                     vtm[:, cc * HD + h * 64: cc * HD + (h + 1) * 64],
                     kqb[:, 5 * HD + h * 64: 5 * HD + (h + 1) * 64],
                     "pb3", [f"vtm{cc}_{pr}" for pr in range(8)] + ["kq5"],
                     tile_position=(0, 64 * (h % 2)))
            for hf in range(2):
                P.mm(pb[2][0:1, :], onesc[:, 0:1],
                     rhotm[:, cc * HD + hf * 512: cc * HD + (hf + 1) * 512],
                     "pb2", ["onesc"] + rkeys)
                P.op("s", lambda e, o=qcrow[0:1, hf * 512:(hf + 1) * 512], i=pb[2][0:1, :]:
                     e.activation(o, i, ACT.Exp, scale=-1.0), "qcrow", ("pb2",))
            ev = qcrow[:, :].rearrange("o (a b) -> o a b", b=64)[:, 0::2, :]
            od = qcrow[:, :].rearrange("o (a b) -> o a b", b=64)[:, 1::2, :]
            P.mm(pb[1][:, :], onesr[0:1, 128:256], ev, "pb1", ("onesr", "qcrow"),
                 start=True, stop=False)
            P.mm(pb[1][:, :], onesr[0:1, 256:384], od, "pb1", ("onesr", "qcrow"),
                 start=False, stop=True)
            P.op("v", lambda e: e.tensor_tensor(sigsc[:, :], sig[:, :], pb[1][:, :], ALU.mult),
                 "sigsc", ("sig", "pb1"))
            P.op("v", lambda e: e.tensor_tensor(sig[:, :], sigsc[:, :], pb[3][:, :], ALU.add),
                 "sig", ("sigsc", "pb3"))
            P.op("v", lambda e: e.tensor_tensor(qtr, qtr, qcrow[:, :], ALU.mult),
                 "qtr", ("qtr", "qcrow"))
            P.op("v", lambda e: e.tensor_tensor(rqr, rqr, qcrow[:, :], ALU.mult),
                 "rqr", ("rqr", "qcrow"))

    P.dma("q", io["o_sig"][:, :], sig[:, :], "osig", ("sig",))
    P.dma("q", io["o_qtot"][:, :], qtr, "oqt", ("qtr",))
    if not io.get("fused"):
        P.finish()


def build_p2(T):
    NTT = max(1, T // 512)
    TS = T // NTT
    CPT = TS // C
    FT = min(T, 512)
    NFT = T // FT
    FO = 4096
    nc = bass.Bass("TRN2", target_bir_lowering=False, debug=False)
    dt = nc.dram_tensor
    io = {}
    io["xT"] = dt("xT", [E, T + 1], F32, kind="ExternalInput").ap()
    io["rwkv"] = dt("rwkv", [T, HD], BF16, kind="ExternalInput").ap()
    io["qglob"] = dt("qglob", [T, HD], BF16, kind="ExternalInput").ap()
    io["rfmH"] = dt("rfmH", [E, T], BF16, kind="ExternalInput").ap()
    io["dd5H"] = dt("dd5H", [E, T], BF16, kind="ExternalInput").ap()
    io["sigAll"] = dt("sigAll", [7 * 128, 512], F32, kind="ExternalInput").ap()
    io["qtAll"] = dt("qtAll", [7, HD], F32, kind="ExternalInput").ap()
    io["gateW"] = dt("gateW", [E, E], F32, kind="ExternalInput").ap()
    io["outW"] = dt("outW", [E, E], F32, kind="ExternalInput").ap()
    io["outB"] = dt("outB", [128, 8], F32, kind="ExternalInput").ap()
    io["gnGB"] = dt("gnGB", [1, 2 * HD], F32, kind="ExternalInput").ap()
    io["WfcT"] = dt("WfcT", [E, FO], F32, kind="ExternalInput").ap()
    io["WprojT"] = dt("WprojT", [FO, E], F32, kind="ExternalInput").ap()
    io["onesC"] = dt("onesC", [128, 4], F32, kind="ExternalInput").ap()
    io["onesR"] = dt("onesR", [1, 384], F32, kind="ExternalInput").ap()
    io["idnC"] = dt("idnC", [128, 128], F32, kind="ExternalInput").ap()
    io["o_out"] = dt("o_out", [E, T], F32, kind="ExternalOutput").ap()

    st = ExitStack()
    with st:
        sb = lambda n_, s_, d_: st.enter_context(nc.sbuf_tensor(n_, s_, d_))
        psm = lambda n_, s_, d_=F32: st.enter_context(nc.psum_tensor(n_, s_, d_))
        B = {}
        B["wfc"] = sb("s_wfc", [128, 8 * FO], BF16)
        B["onesc"] = sb("s_onesc", [128, 4], BF16)
        B["onesr"] = sb("s_onesr", [1, 384], BF16)
        B["onesrF"] = sb("s_onesrF", [1, 128], F32)
        B["out1"] = sb("s_out1", [128, 8 * T], BF16)
        for i in range(6):
            B[f"pb{i}"] = psm(f"s_pb{i}", [128, 512])
        B["pst"] = psm("s_pst", [1, 512])
        B["pst2"] = psm("s_pst2", [1, 512])

        sems = ExitStack()
        with sems, nc.Block() as blk:
            P = Prog(nc, sems)

            @blk.sync
            def _(_q):
                ast = ExitStack()
                with ast:
                    sba = lambda n_, s_, d_: ast.enter_context(nc.sbuf_tensor(n_, s_, d_))
                    B["gate"] = sba("s_gate", [128, 8 * E], BF16)
                    B["outw"] = sba("s_outw", [128, 8 * E], BF16)
                    B["outb"] = sba("s_outb", [128, 8], F32)
                    B["gnrep"] = sba("s_gnrep", [128, 2 * HD], BF16)
                    B["sigI"] = sba("s_sigI", [128, 512], F32)
                    B["sigB"] = sba("s_sigB", [128, 512], BF16)
                    B["xt"] = sba("s_xt", [128, 8 * TS], BF16)
                    B["rwkvc"] = sba("s_rwkvc", [128, HD], F32)
                    B["rwkvb"] = [sba("s_rwkvb" + str(i2), [128, HD], BF16)
                                      for i2 in range(2)]
                    B["qgl"] = [sba("s_qgl" + str(i2), [128, HD], BF16)
                                      for i2 in range(2)]
                    B["stats"] = sba("s_stats", [128, 128], F32)
                    B["nrm"] = sba("s_nrm", [128, HD], F32)
                    B["nrmb"] = sba("s_nrmb", [128, HD], BF16)
                    B["gsil"] = sba("s_gsil", [128, HD], BF16)
                    B["prod"] = [sba("s_prod" + str(i2), [128, HD], F32)
                                      for i2 in range(2)]
                    B["idnF"] = sba("s_idnF", [128, 128], F32)
                    B["idnB"] = sba("s_idnB", [128, 128], BF16)
                    B["prodT"] = sba("s_prodT", [128, 8 * TS], BF16)
                    B["dd5t"] = sba("s_dd5t", [128, 8 * TS], BF16)
                    B["rfmt"] = sba("s_rfmt", [128, 8 * TS], BF16)
                    B["vres"] = sba("s_vres", [128, 8 * TS], BF16)
                    B["rowst"] = sba("s_rowst", [128, TS], F32)
                    emit_p2_attn(P, nc, io, B, T, NTT, TS, CPT, FO)
                P.barrier()
                fst = ExitStack()
                with fst:
                    sbf = lambda n_, s_, d_: fst.enter_context(nc.sbuf_tensor(n_, s_, d_))
                    B["wpj"] = sbf("s_wpj", [128, 32 * E], BF16)
                    B["hbuf"] = sbf("s_hbuf", [128, 32 * FT], BF16)
                    B["relu"] = sbf("s_relu", [128, FT], F32)
                    B["sacc"] = sbf("s_sacc", [128, FT], BF16)
                    B["vres2"] = sbf("s_vres2", [128, 8 * FT], BF16)
                    B["rowst2"] = sbf("s_rowst2", [128, FT], F32)

                    emit_p2_ffn(P, nc, io, B, T, NTT, TS, FT, NFT, FO)
    return nc


def emit_p2_attn(P, nc, io, B, T, NTT, TS, CPT, FO):
    gate, outw, outb = B["gate"], B["outw"], B["outb"]
    wfc = B["wfc"]
    onesc, onesr = B["onesc"], B["onesr"]
    gnrep = B["gnrep"]
    sigI, sigB = B["sigI"], B["sigB"]
    xt, rwkvc, rwkvb, qgl = B["xt"], B["rwkvc"], B["rwkvb"], B["qgl"]
    tmpc = B["gsil"]
    stats, nrm, nrmb = B["stats"], B["nrm"], B["nrmb"]
    sq = nrm
    gsil, prod, prodT = B["gsil"], B["prod"], B["prodT"]
    dd5t, rfmt, vres = B["dd5t"], B["rfmt"], B["vres"]
    rowst = B["rowst"]
    out1 = B["out1"]
    pb = [B[f"pb{i}"] for i in range(6)]
    pst, pst2 = B["pb4"], B["pb5"]

    def ld3(sbuf, hbm, wkey, csz):
        P.dma("gq", sbuf[:, :].rearrange("p (a c) -> p a c", c=csz),
              hbm.rearrange("(a p) c -> p a c", p=128), wkey)

    ld3(gate, io["gateW"], "gate", E)
    ld3(outw, io["outW"], "outw", E)
    P.dma("q", outb[:, :], io["outB"][:, :], "outb")
    P.dma("gq", onesc[:, :], io["onesC"][:, :], "onesc")
    P.dma("gq", onesr[:, :], io["onesR"][:, :], "onesr")
    P.dma("q", B["onesrF"][:, :], io["onesR"][0:1, 0:128], "onesrF")
    P.dma("q", B["idnF"][:, :], io["idnC"][:, :], "idnF")
    P.dma("gq", B["idnB"][:, :], io["idnC"][:, :], "idn")
    gnrow = B["rwkvc"][0:1, :].bitcast(BF16)
    P.dma("gq", gnrow, io["gnGB"][:, :], ("rwcA", "rwcB"))
    for c8 in range(8):
        P.dma("gq", xt[:, c8 * TS:(c8 + 1) * TS],
              io["xT"][c8 * 128:(c8 + 1) * 128, 1:TS + 1], f"xt{c8}")
        P.dma("q", dd5t[:, c8 * TS:(c8 + 1) * TS],
              io["dd5H"][c8 * 128:(c8 + 1) * 128, 0:TS], f"dd5t{c8}")
        P.dma("q", rfmt[:, c8 * TS:(c8 + 1) * TS],
              io["rfmH"][c8 * 128:(c8 + 1) * 128, 0:TS], f"rfmt{c8}")
    if io.get("collectives"):
        io["collectives"]()
    for gb in range(2):
        for hf in range(2):
            P.mm(pb[4][:, :], onesr[0:1, 0:128],
                 gnrow[0:1, gb * HD + hf * 512: gb * HD + (hf + 1) * 512],
                 "pb4", ("onesr", "rwcA", "rwcB"))
            P.op("v", lambda e, o=gnrep[:, gb * HD + hf * 512: gb * HD + (hf + 1) * 512],
                 i=pb[4][:, :]: e.tensor_copy(o, i), "gnrep", ("pb4",))
    sigT = rwkvc[:, 0:512]
    sigJs = [(rwkvc[:, 512:1024], "rwcB"), (nrm[:, 512:1024], "nrm")]
    qts = [(qgl[0][0:1, 0:HD], "qgl0"), (qgl[1][0:1, 0:HD], "qgl1")]
    P.op("v", lambda e: e.memset(sigI[:, :], 0.0), "sigI", ())
    nslots = NCORES if io.get("fused") else 7
    fused = bool(io.get("fused"))

    def fold_load(j):
        p = j % 2
        sj, sjk = sigJs[p]
        qt, qtk = qts[p]
        if fused:
            P.dma("q", sj, io["sigAll"][j * 128:(j + 1) * 128, :], sjk,
                  r=("ccsigG",), grp=f"qsj{p}")
            P.dma("gq", qt, io["qtAll"][j:j + 1, :], qtk, r=("ccqtG",),
                  grp=f"gqt{p}")
        else:
            P.dma("q", sj, io["sigAll"][j * 128:(j + 1) * 128, :], sjk,
                  grp=f"qsj{p}")
            P.dma("gq", qt, io["qtAll"][j:j + 1, :], qtk, grp=f"gqt{p}")

    fold_load(0)
    for jj in range(nslots):
        p = jj % 2
        sj, sjk = sigJs[p]
        qt, qtk = qts[p]
        if jj + 1 < nslots:
            fold_load(jj + 1)
        if fused:
            cm = B["cmask"]
            P.op("v", lambda e, o=sj: e.tensor_single_scalar(o, o, cm[:, jj:jj + 1],
                                                             ALU.mult), sjk, (sjk, "cmask"))
            P.op("v", lambda e, o=qt: e.tensor_scalar(o, o, cm[0:1, jj:jj + 1],
                                                      cm[0:1, 8 + jj:9 + jj], ALU.mult, ALU.add),
                 qtk, (qtk, "cmask"))
        ev = qt.rearrange("o (a b) -> o a b", b=64)[:, 0::2, :]
        od = qt.rearrange("o (a b) -> o a b", b=64)[:, 1::2, :]
        P.mm(pb[5][:, :], onesr[0:1, 128:256], ev, "pb5", ("onesr", qtk),
             start=True, stop=False)
        P.mm(pb[5][:, :], onesr[0:1, 256:384], od, "pb5", ("onesr", qtk),
             start=False, stop=True)
        P.op("v", lambda e: e.tensor_tensor(sigT, sigI[:, :], pb[5][:, :], ALU.mult),
             "rwcA", ("sigI", "pb5"))
        P.op("v", lambda e, a=sj: e.tensor_tensor(sigI[:, :], sigT, a, ALU.add),
             "sigI", ("rwcA", sjk))
    P.op("v", lambda e: e.tensor_copy(sigB[:, :], sigI[:, :]), "sigB", ("sigI",))

    def chunk_front(tau, cc):
        p = cc % 2
        rwb, qg = rwkvb[p], qgl[p]
        rwc, tmp = rwkvc, tmpc
        st, nr, nrb = stats, nrm, nrmb
        sq_, gs, pr = sq, gsil, prod[p]
        ch = tau * CPT + cc
        P.dma("q", rwb[:, :], io["rwkv"][ch * 128:(ch + 1) * 128, :], f"rwb{p}",
              grp=f"qrwb{p}")
        P.dma("q", qg[:, :], io["qglob"][ch * 128:(ch + 1) * 128, :], f"qgl{p}",
              grp=f"qqgl{p}")
        for h in range(16):
            bi = 2 + h // 8
            rs = slice(64 * (h % 2), 64 * (h % 2) + 64)
            cols = slice((h // 2) * TS + cc * C, (h // 2) * TS + (cc + 1) * C)
            P.mm(pb[bi][:, (h % 8) * 64:(h % 8) * 64 + 64], rfmt[rs, cols],
                 sigB[rs, (h // 2) * 64:(h // 2) * 64 + 64],
                 f"pb{bi}", (f"rfmt{h // 2}", "sigB"))
        for b in range(2):
            P.op("v", lambda e, o=tmp[:, b * 512:(b + 1) * 512], x=pb[2 + b][:, :],
                 q=qg[:, b * 512:(b + 1) * 512]: e.tensor_tensor(o, x, q, ALU.mult),
                 "gsil", (f"pb{2 + b}", f"qgl{p}"))
        P.op("v", lambda e: e.tensor_tensor(rwc[:, :], rwb[:, :], tmp[:, :], ALU.add),
             ("rwc", "rwcA", "rwcB"), (f"rwb{p}", "gsil"))
        r3 = rwc[:, :].rearrange("p (h d) -> p h d", d=D)
        P.op("v", lambda e, o=st[:, 0:16]: e.tensor_reduce(o, r3, AXL.X, ALU.add),
             "st_sum", ("rwc",))
        P.op("s", lambda e: e.activation(sq_[:, :], rwc[:, :], ACT.Square),
             "nrm", ("rwc",))
        s3 = sq_[:, :].rearrange("p (h d) -> p h d", d=D)
        P.op("v", lambda e, o=st[:, 16:32]: e.tensor_reduce(o, s3, AXL.X, ALU.add),
             "st_sq", ("nrm",))
        P.op("v", lambda e, o=st[:, 32:48], i=st[:, 0:16]:
             e.tensor_scalar_mul(o, i, 1.0 / D), "st_m", ("st_sum",))
        P.op("v", lambda e, o=st[:, 48:64], i=st[:, 16:32]:
             e.tensor_scalar_mul(o, i, 1.0 / D), "st_msq", ("st_sq",))
        P.op("v", lambda e, o=st[:, 64:80], a=st[:, 32:48], b=st[:, 32:48]:
             e.tensor_tensor(o, a, b, ALU.mult), "st_m2", ("st_m",))
        P.op("v", lambda e, o=st[:, 80:96], a=st[:, 48:64], b=st[:, 64:80]:
             e.tensor_tensor(o, a, b, ALU.subtract), "st_var",
             ("st_msq", "st_m2"))
        P.op("v", lambda e, o=st[:, 80:96], i=st[:, 80:96]:
             e.tensor_scalar_add(o, i, GN_EPS), "st_var", ("st_var",))
        P.op("s", lambda e, o=st[:, 96:112], i=st[:, 80:96]:
             e.activation(o, i, ACT.Sqrt), "st_sd", ("st_var",))
        P.op("v", lambda e, o=st[:, 112:128], i=st[:, 96:112]:
             e.reciprocal(o, i), "st_rstd", ("st_sd",))
        m3 = st[:, 32:48].unsqueeze(2).broadcast_to([128, 16, D])
        rs3 = st[:, 112:128].unsqueeze(2).broadcast_to([128, 16, D])
        n3 = nr[:, :].rearrange("p (h d) -> p h d", d=D)
        P.op("v", lambda e: e.tensor_tensor(n3, r3, m3, ALU.subtract), "nrm",
             ("rwc", "st_m"))
        P.op("v", lambda e: e.tensor_tensor(n3, n3, rs3, ALU.mult), "nrm",
             ("nrm", "st_rstd"))
        P.op("v", lambda e, o=nr[:, :]: e.tensor_tensor(o, o, gnrep[:, 0:HD], ALU.mult),
             "nrm", ("nrm", "gnrep"))
        P.op("v", lambda e, o=nrb[:, :], i=nr[:, :]:
             e.tensor_tensor(o, i, gnrep[:, HD:2 * HD], ALU.add),
             "nrmb", ("nrm", "gnrep"))
        for ot in range(2):
            for c8 in range(8):
                P.mm(pb[ot][:, :], dd5t[:, c8 * TS + cc * C: c8 * TS + (cc + 1) * C],
                     gate[:, c8 * E + ot * 512: c8 * E + (ot + 1) * 512],
                     f"pb{ot}", (f"dd5t{c8}", "gate"), start=(c8 == 0), stop=(c8 == 7))
            P.op("s", lambda e, o=gs[:, ot * 512:(ot + 1) * 512], i=pb[ot][:, :]:
                 e.activation(o, i, ACT.Silu), "gsil", (f"pb{ot}",))
        P.op("v", lambda e: e.tensor_tensor(pr[:, :], nrb[:, :], gs[:, :], ALU.mult),
             f"prod{p}", ("nrmb", "gsil"))

    def chunk_back(tau, cc):
        p = cc % 2
        pr = prod[p]
        for c8 in range(8):
            bi = 4 + (c8 % 2)
            P.op("t", lambda e, o=pb[bi][:, 0:128], i=pr[:, c8 * 128:(c8 + 1) * 128]:
                 e.transpose(o, i, B["idnF"][:, :]), f"pb{bi}", (f"prod{p}", "idnF"))
            P.op("s", lambda e, o=prodT[:, c8 * TS + cc * C: c8 * TS + (cc + 1) * C],
                 i=pb[bi][:, 0:128]: e.copy(o, i), f"prodT{c8}", (f"pb{bi}",))

    ld3(wfc, io["WfcT"], "wfc", FO)
    for tau in range(NTT):
        t0 = tau * TS
        if tau > 0:
            for c8 in range(8):
                P.dma("gq", xt[:, c8 * TS:(c8 + 1) * TS],
                      io["xT"][c8 * 128:(c8 + 1) * 128, t0 + 1:t0 + TS + 1], f"xt{c8}")
                P.dma("q", dd5t[:, c8 * TS:(c8 + 1) * TS],
                      io["dd5H"][c8 * 128:(c8 + 1) * 128, t0:t0 + TS], f"dd5t{c8}")
                P.dma("q", rfmt[:, c8 * TS:(c8 + 1) * TS],
                      io["rfmH"][c8 * 128:(c8 + 1) * 128, t0:t0 + TS], f"rfmt{c8}")
        chunk_front(tau, 0)
        for cc in range(CPT):
            if cc + 1 < CPT:
                chunk_front(tau, cc + 1)
            chunk_back(tau, cc)
        for o8 in range(8):
            bi = o8 % 2
            for c8 in range(8):
                P.mm(pb[bi][:, 0:TS], outw[:, c8 * E + o8 * 128: c8 * E + (o8 + 1) * 128],
                     prodT[:, c8 * TS:(c8 + 1) * TS], f"pb{bi}",
                     (f"prodT{c8}", "outw"), start=(c8 == 0), stop=(c8 == 7))
            P.op("v", lambda e, o=vres[:, o8 * TS:(o8 + 1) * TS], i=pb[bi][:, 0:TS],
                 b=outb[:, o8:o8 + 1], x=xt[:, o8 * TS:(o8 + 1) * TS]:
                 e.scalar_tensor_tensor(o, i, b, x, ALU.add, ALU.add),
                 f"vres{o8}", (f"pb{bi}", "outb", f"xt{o8}"))
        v2f = prodT
        for c8 in range(8):
            P.op("s", lambda e, o=v2f[:, c8 * TS:(c8 + 1) * TS],
                 i=vres[:, c8 * TS:(c8 + 1) * TS]: e.activation(o, i, ACT.Square),
                 f"prodT{c8}", (f"vres{c8}",))
        for c8 in range(8):
            P.mm(pst[0:1, 0:TS], onesc[:, 0:1], vres[:, c8 * TS:(c8 + 1) * TS],
                 "pb4", ["onesc"] + [f"vres{i}" for i in range(8)],
                 start=(c8 == 0), stop=(c8 == 7))
        for c8 in range(8):
            P.mm(pst2[0:1, 0:TS], onesc[:, 0:1], v2f[:, c8 * TS:(c8 + 1) * TS],
                 "pb5", ["onesc"] + [f"prodT{i}" for i in range(8)],
                 start=(c8 == 0), stop=(c8 == 7))
        ln_rows(P, pst, pst2, rowst[0:1, 0:TS], sigI[0:1, 0:TS],
                "lm1", "sigI", TS, LN_EPS)
        for mb, (rsl, rk) in enumerate(((rowst[0:1, 0:TS], "lm1"),
                                        (sigI[0:1, 0:TS], "sigI"))):
            P.mm(pb[2 + mb][:, 0:TS], B["onesrF"][0:1, 0:128], rsl,
                 f"pb{2 + mb}", (rk, "onesrF"))
        for c8 in range(8):
            P.op("v", lambda e, o=vres[:, c8 * TS:(c8 + 1) * TS],
                 a=vres[:, c8 * TS:(c8 + 1) * TS], b=pb[2][:, 0:TS]:
                 e.tensor_tensor(o, a, b, ALU.subtract), f"vres{c8}", (f"vres{c8}", "pb2"))
            P.op("v", lambda e, o=out1[:, c8 * T + t0: c8 * T + t0 + TS],
                 a=vres[:, c8 * TS:(c8 + 1) * TS], b=pb[3][:, 0:TS]:
                 e.tensor_tensor(o, a, b, ALU.mult), f"out1_{tau}_{c8}",
                 (f"vres{c8}", "pb3"))



def emit_p2_ffn(P, nc, io, B, T, NTT, TS, FT, NFT, FO):
    wfc, wpj = B["wfc"], B["wpj"]
    onesc, onesr = B["onesc"], B["onesr"]
    out1, hbuf = B["out1"], B["hbuf"]
    sacc, vres2 = B["sacc"], B["vres2"]
    rowst2 = B["rowst2"]
    relu = B["relu"]
    pb = [B[f"pb{i}"] for i in range(6)]
    pst, pst2 = B["pb4"], B["pb5"]

    def ld3(sbuf, hbm, wkey, csz):
        P.dma("gq", sbuf[:, :].rearrange("p (a c) -> p a c", c=csz),
              hbm.rearrange("(a p) c -> p a c", p=128), wkey)

    ld3(wpj, io["WprojT"], "wpj", E)

    def tail(ft):
        t0 = ft * FT
        v22f = hbuf[:, 24 * FT:32 * FT]
        hk8 = [f"h{24 + i}" for i in range(8)]
        for c8 in range(8):
            sl2 = slice(c8 * FT, (c8 + 1) * FT)
            P.op("s", lambda e, o=v22f[:, sl2], i=vres2[:, sl2]:
                 e.activation(o, i, ACT.Square), f"h{24 + c8}", (f"vr2_{c8}",))
        yield
        for c8 in range(8):
            sl2 = slice(c8 * FT, (c8 + 1) * FT)
            P.mm(pst[0:1, 0:FT], onesc[:, 0:1], vres2[:, sl2],
                 "pb4", ["onesc"] + [f"vr2_{i}" for i in range(8)],
                 start=(c8 == 0), stop=(c8 == 7))
        yield
        for c8 in range(8):
            sl2 = slice(c8 * FT, (c8 + 1) * FT)
            P.mm(pst2[0:1, 0:FT], onesc[:, 0:1], v22f[:, sl2],
                 "pb5", ["onesc"] + hk8, start=(c8 == 0), stop=(c8 == 7))
        yield
        ln_rows(P, pst, pst2, rowst2[0:1, 0:FT], relu[0:1, 0:FT],
                "lm2", "relu", FT, LN_EPS)
        for mb, (rsl, rk2) in enumerate(((rowst2[0:1, 0:FT], "lm2"),
                                         (relu[0:1, 0:FT], "relu"))):
            bk = (3, 5)[mb]
            P.mm(pb[bk][:, 0:FT], B["onesrF"][0:1, 0:128], rsl,
                 f"pb{bk}", (rk2, "onesrF"))
        yield
        for c8 in range(8):
            sl = slice(c8 * FT, (c8 + 1) * FT)
            osl = rowst2[:, 0:FT]
            P.op("v", lambda e, o=vres2[:, sl], a=vres2[:, sl], b=pb[3][:, 0:FT]:
                 e.tensor_tensor(o, a, b, ALU.subtract), f"vr2_{c8}", (f"vr2_{c8}", "pb3"))
            P.op("v", lambda e, o=osl, a=vres2[:, sl], b=pb[5][:, 0:FT]:
                 e.tensor_tensor(o, a, b, ALU.mult), "lm2", (f"vr2_{c8}", "pb5"))
            P.dma("q", io["o_out"][c8 * 128:(c8 + 1) * 128, t0:t0 + FT], osl,
                  f"oo_{ft}_{c8}", ("lm2",), grp="qoo")
            if c8 % 2 == 1:
                yield

    tgen = None
    for ft in range(NFT):
        t0 = ft * FT
        o1keys = [f"out1_{t0 // TS}_{c8}" for c8 in range(8)]
        for f8 in range(32):
            bi = f8 % 3
            for c8 in range(8):
                P.mm(pb[bi][:, 0:FT], wfc[:, c8 * FO + f8 * 128: c8 * FO + (f8 + 1) * 128],
                     out1[:, c8 * T + t0: c8 * T + t0 + FT], f"pb{bi}",
                     ["wfc"] + o1keys, start=(c8 == 0), stop=(c8 == 7))
            P.op("s", lambda e, o=relu[:, 0:FT], i=pb[bi][:, 0:FT]:
                 e.activation(o, i, ACT.Relu), "relu", (f"pb{bi}",))
            P.op("s", lambda e, o=hbuf[:, f8 * FT:(f8 + 1) * FT], i=relu[:, 0:FT]:
                 e.activation(o, i, ACT.Square), f"h{f8}", ("relu",))
            if tgen is not None and f8 % 3 == 2:
                next(tgen, None)
        if tgen is not None:
            for _ in tgen:
                pass
        hkeys = [f"h{i}" for i in range(32)]
        for o8 in range(8):
            bi = 3 + o8 % 2
            for f8 in range(32):
                P.mm(pb[bi][:, 0:FT], wpj[:, f8 * E + o8 * 128: f8 * E + (o8 + 1) * 128],
                     hbuf[:, f8 * FT:(f8 + 1) * FT], f"pb{bi}",
                     ["wpj"] + hkeys, start=(f8 == 0), stop=(f8 == 31))
            sl = slice(o8 * FT, (o8 + 1) * FT)
            P.op("v", lambda e, o=sacc[:, 0:FT], i=pb[bi][:, 0:FT]:
                 e.tensor_single_scalar(o, i, THRESH[0], ALU.is_gt),
                 "sacc", (f"pb{bi}",))
            for jt in range(1, 5):
                P.op("v",
                     lambda e, o=sacc[:, 0:FT], i=pb[bi][:, 0:FT], s_=sacc[:, 0:FT], th=THRESH[jt]:
                     e.scalar_tensor_tensor(o, i, th, s_, ALU.is_gt, ALU.add),
                     "sacc", (f"pb{bi}", "sacc"))
            P.op("v", lambda e, o=vres2[:, sl], i=sacc[:, 0:FT],
                 b=out1[:, o8 * T + t0: o8 * T + t0 + FT]:
                 e.scalar_tensor_tensor(o, i, 0.2, b, ALU.mult, ALU.add),
                 f"vr2_{o8}", tuple(["sacc"] + o1keys))
        tgen = tail(ft)
    for _ in tgen:
        pass

    P.finish()


def ln_rows(P, pst, pst2, a, b, akey, bkey, W, eps):
    t = pst[0:1, 0:W]
    P.op("v", lambda e: e.tensor_scalar_mul(a, t, 1.0 / E), akey, ("pb4",))
    P.op("v", lambda e: e.tensor_scalar_mul(b, pst2[0:1, 0:W], 1.0 / E), bkey, ("pb5",))
    P.op("v", lambda e: e.scalar_tensor_tensor(t, a, 1.0, a, ALU.mult, ALU.mult),
         "pb4", (akey,))
    P.op("v", lambda e: e.tensor_tensor(b, b, t, ALU.subtract), bkey, (bkey, "pb4"))
    P.op("v", lambda e: e.tensor_scalar_add(b, b, eps), bkey, (bkey,))
    P.op("s", lambda e: e.activation(b, b, ACT.Sqrt), bkey, (bkey,))
    P.op("v", lambda e: e.reciprocal(b, b), bkey, (bkey,))


def prep_shared(inputs):
    lam = np.asarray(inputs["L0_attn_lora_lambda"], np.float32)
    Aw = np.asarray(inputs["L0_attn_lora_A_w"], np.float32)
    Ab = np.asarray(inputs["L0_attn_lora_A_b"], np.float32)
    Bw = np.asarray(inputs["L0_attn_lora_B_w"], np.float32)
    Bb = np.asarray(inputs["L0_attn_lora_B_b"], np.float32)
    mhd = np.asarray(inputs["L0_attn_mhd_w"], np.float32)
    wkvA = np.asarray(inputs["L0_attn_wkv_A"], np.float32)
    wkvB = np.asarray(inputs["L0_attn_wkv_B"], np.float32)
    wlam = np.asarray(inputs["L0_attn_wkv_lamb"], np.float32)

    laA = np.zeros((E, 3 * 128), np.float32)
    laAb = np.zeros((128, 3), np.float32)
    for p in range(3):
        laA[:, p * 128 + 0:p * 128 + 32] = Aw[2 * p]
        laA[:, p * 128 + 64:p * 128 + 96] = Aw[2 * p + 1]
        laAb[0:32, p] = Ab[2 * p]
        laAb[64:96, p] = Ab[2 * p + 1]
    laB = np.zeros((3 * 128, E), np.float32)
    for p in range(3):
        for half in range(2):
            g = 2 * p + half
            r0 = p * 128 + 64 * half
            laB[r0:r0 + 32] = Bw[g]
            laB[r0 + 32] = lam[g]
            laB[r0 + 33] = Bb[g]
    mhdBD = np.zeros((5 * 8 * 128, 128), np.float32)
    for g in range(5):
        for p in range(8):
            blk = mhdBD[(g * 8 + p) * 128:(g * 8 + p + 1) * 128]
            blk[0:64, 0:64] = mhd[g, 2 * p]
            blk[64:128, 64:128] = mhd[g, 2 * p + 1]
    wkvABD = np.zeros((8 * 128, 128), np.float32)
    wkvBBD = np.zeros((8 * 128, 128), np.float32)
    for p in range(8):
        wkvABD[p * 128:p * 128 + 64, 0:64] = wkvA[2 * p]
        wkvABD[p * 128 + 64:(p + 1) * 128, 64:128] = wkvA[2 * p + 1]
        wkvBBD[p * 128:p * 128 + 64, 0:64] = wkvB[2 * p]
        wkvBBD[p * 128 + 64:(p + 1) * 128, 64:128] = wkvB[2 * p + 1]
    lambP = np.zeros((1, 8 * 128), np.float32)
    for p in range(8):
        lambP[0, p * 128:p * 128 + 64] = wlam[2 * p]
        lambP[0, p * 128 + 64:(p + 1) * 128] = wlam[2 * p + 1]
    onesC = np.zeros((128, 4), np.float32)
    onesC[:, 0] = 1.0
    onesC[0:64, 1] = 1.0
    onesC[64:128, 2] = 1.0
    onesR = np.zeros((1, 384), np.float32)
    onesR[0, 0:128] = 1.0
    onesR[0, 128:192] = 1.0
    onesR[0, 320:384] = 1.0
    muX = np.ascontiguousarray(
        np.asarray(inputs["L0_attn_mu_x"], np.float32).reshape(8, 128).T)
    gnGB = np.concatenate([np.asarray(inputs["L0_attn_gn_gamma"], np.float32),
                           np.asarray(inputs["L0_attn_gn_beta"], np.float32)])[None, :]
    outB = np.ascontiguousarray(
        np.asarray(inputs["L0_attn_out_b"], np.float32).reshape(8, 128).T)
    return dict(
        laA=laA, laAb=laAb, laB=laB, mhdBD=mhdBD, wkvABD=wkvABD, wkvBBD=wkvBBD,
        lambP=lambP, CLmats=make_matrices(), masksC=make_masks(), muX=muX,
        onesC=onesC, onesR=onesR, idnC=np.eye(128, dtype=np.float32),
        gateW=np.ascontiguousarray(np.asarray(inputs["L0_attn_gate_w"], np.float32)),
        outW=np.ascontiguousarray(np.asarray(inputs["L0_attn_out_w"], np.float32)),
        outB=outB, gnGB=gnGB,
        WfcT=np.ascontiguousarray(np.asarray(inputs["L0_ffn_W_fc"], np.float32).T),
        WprojT=np.ascontiguousarray(np.asarray(inputs["L0_ffn_W_proj"], np.float32).T),
    )


P1_KEYS = ["laA", "laAb", "laB", "mhdBD", "wkvABD", "wkvBBD", "lambP",
           "CLmats", "masksC", "muX", "onesC", "onesR", "idnC"]
P2_KEYS = ["gateW", "outW", "outB", "gnGB", "WfcT", "WprojT", "onesC", "onesR", "idnC"]


def kernel_2launch(**inputs):
    x = np.asarray(inputs["L0_attn_x"], np.float32)[0]
    Tfull = x.shape[0]
    Tc = Tfull // NCORES
    shared = prep_shared(inputs)
    xT = np.ascontiguousarray(x.T)
    xTp = np.concatenate([np.zeros((E, 1), np.float32), xT], 1)

    nc1 = build_p1(Tc)
    in1 = []
    for cid in range(NCORES):
        m = {k: shared[k] for k in P1_KEYS}
        m["xT"] = np.ascontiguousarray(xTp[:, cid * Tc: cid * Tc + Tc + 1])
        in1.append(m)
    res1 = run_bass_kernel_spmd(nc1, in1, core_ids=list(range(NCORES)))
    r1 = res1.results

    nc2 = build_p2(Tc)
    in2 = []
    zsig = np.zeros((128, 512), np.float32)
    ones_qt = np.ones((HD,), np.float32)
    for cid in range(NCORES):
        m = {k: shared[k] for k in P2_KEYS}
        m["xT"] = in1[cid]["xT"]
        m["rwkv"] = r1[cid]["o_rwkv"]
        m["qglob"] = r1[cid]["o_qglob"]
        m["rfmH"] = r1[cid]["o_rfm"]
        m["dd5H"] = r1[cid]["o_dd5"]
        sigs, qts = [], []
        for j in range(7):
            if j < cid:
                sigs.append(np.asarray(r1[j]["o_sig"], np.float32))
                qts.append(np.asarray(r1[j]["o_qtot"], np.float32).reshape(HD))
            else:
                sigs.append(zsig)
                qts.append(ones_qt)
        m["sigAll"] = np.ascontiguousarray(np.concatenate(sigs, 0))
        m["qtAll"] = np.ascontiguousarray(np.stack(qts, 0))
        in2.append(m)
    res2 = run_bass_kernel_spmd(nc2, in2, core_ids=list(range(NCORES)))
    outs = [np.asarray(r["o_out"], np.float32).T for r in res2.results]
    return np.ascontiguousarray(np.concatenate(outs, 0))[None, :, :]


def build_fused(T):
    NT = max(1, T // 512)
    TS = T // NT
    CPT = TS // C
    NCH = T // C
    NTT = NT
    FT = min(T, 512)
    NFT = T // FT
    FO = 4096
    nc = bass.Bass("TRN2", target_bir_lowering=False, debug=False)
    dt = nc.dram_tensor
    io = {}
    io["xT"] = dt("xT", [E, T + 1], F32, kind="ExternalInput").ap()
    io["laA"] = dt("laA", [E, 3 * 128], F32, kind="ExternalInput").ap()
    io["laAb"] = dt("laAb", [128, 3], F32, kind="ExternalInput").ap()
    io["laB"] = dt("laB", [3 * 128, E], F32, kind="ExternalInput").ap()
    io["mhdBD"] = dt("mhdBD", [5 * 8 * 128, 128], F32, kind="ExternalInput").ap()
    io["wkvABD"] = dt("wkvABD", [8 * 128, 128], F32, kind="ExternalInput").ap()
    io["wkvBBD"] = dt("wkvBBD", [8 * 128, 128], F32, kind="ExternalInput").ap()
    io["lambP"] = dt("lambP", [1, 8 * 128], F32, kind="ExternalInput").ap()
    io["CLmats"] = dt("CLmats", [8 * 128, 128], F32, kind="ExternalInput").ap()
    io["masksC"] = dt("masksC", [3 * 128, 128], F32, kind="ExternalInput").ap()
    io["muX"] = dt("muX", [128, 8], F32, kind="ExternalInput").ap()
    io["onesC"] = dt("onesC", [128, 4], F32, kind="ExternalInput").ap()
    io["onesR"] = dt("onesR", [1, 384], F32, kind="ExternalInput").ap()
    io["idnC"] = dt("idnC", [128, 128], F32, kind="ExternalInput").ap()
    io["gateW"] = dt("gateW", [E, E], F32, kind="ExternalInput").ap()
    io["outW"] = dt("outW", [E, E], F32, kind="ExternalInput").ap()
    io["outB"] = dt("outB", [128, 8], F32, kind="ExternalInput").ap()
    io["gnGB"] = dt("gnGB", [1, 2 * HD], F32, kind="ExternalInput").ap()
    io["WfcT"] = dt("WfcT", [E, FO], F32, kind="ExternalInput").ap()
    io["WprojT"] = dt("WprojT", [FO, E], F32, kind="ExternalInput").ap()
    io["cmaskC"] = dt("cmaskC", [128, 16], F32, kind="ExternalInput").ap()
    io["o_out"] = dt("o_out", [E, T], F32, kind="ExternalOutput").ap()
    rwkvI = dt("rwkvI", [T, HD], BF16, kind="Internal").ap()
    qglobI = dt("qglobI", [T, HD], BF16, kind="Internal").ap()
    rfmI = dt("rfmI", [E, T], BF16, kind="Internal").ap()
    dd5I = dt("dd5I", [E, T], BF16, kind="Internal").ap()
    ccsig = dt("ccsig", [128, 512], F32, kind="Internal").ap()
    ccqt = dt("ccqt", [1, HD], F32, kind="Internal").ap()
    ccsigG = dt("ccsigG", [NCORES * 128, 512], F32, kind="Internal", addr_space="Shared").ap()
    ccqtG = dt("ccqtG", [NCORES, HD], F32, kind="Internal", addr_space="Shared").ap()

    io1 = dict(io)
    io1.update({"o_rwkv": rwkvI, "o_qglob": qglobI, "o_rfm": rfmI, "o_dd5": dd5I,
                "o_sig": ccsig, "o_qtot": ccqt, "fused": True})
    io2 = dict(io)
    io2.update({"rwkv": rwkvI, "qglob": qglobI, "rfmH": rfmI, "dd5H": dd5I,
                "sigAll": ccsigG, "qtAll": ccqtG, "fused": True})

    sems = ExitStack()
    with sems, nc.Block() as blk:
        P = Prog(nc, sems)

        @blk.sync
        def _(_q):
            s1 = ExitStack()
            with s1:
                sb = lambda n_, s_, d_: s1.enter_context(nc.sbuf_tensor(n_, s_, d_))
                psm = lambda n_, s_, d_=F32: s1.enter_context(nc.psum_tensor(n_, s_, d_))
                B = {}
                B["laA"] = sb("s_laA", [128, 8 * 384], BF16)
                B["laAb"] = sb("s_laAb", [128, 3], F32)
                for p in range(3):
                    B[f"laB{p}"] = sb(f"laB{p}", [128, E], BF16)
                B["mhd"] = sb("s_mhd", [128, 40 * 128], BF16)
                B["wkvA"] = sb("s_wkvA", [128, 8 * 128], BF16)
                B["wkvB"] = sb("s_wkvB", [128, 8 * 128], BF16)
                B["lamb"] = sb("s_lamb", [1, 8 * 128], BF16)
                B["mats"] = sb("s_mats", [128, 8 * 128], BF16)
                B["masks"] = sb("s_masks", [128, 3 * 128], BF16)
                B["mu"] = sb("s_mu", [128, 8], F32)
                B["onesc"] = sb("s_onesc", [128, 4], BF16)
                B["onesr"] = sb("s_onesr", [1, 384], BF16)
                B["idn"] = sb("s_idn", [128, 128], BF16)
                B["xt"] = sb("s_xt", [128, 8 * (TS + 1)], F32)
                B["diff"] = sb("s_diff", [128, 8 * TS], BF16)
                B["scr"] = sb("s_scr", [128, 8 * TS], BF16)
                B["zb"] = sb("s_zb", [128, 8 * TS], BF16)
                for p in range(3):
                    B[f"hdn{p}"] = sb(f"hdn{p}", [128, TS], BF16)
                for p in range(2):
                    B[f"ddg{p}"] = sb(f"ddg{p}", [128, 8 * TS], BF16)
                B["vfm"] = sb("s_vfm", [128, 8 * TS], BF16)
                B["rfm"] = sb("s_rfm", [128, 8 * TS], BF16)
                B["ktm"] = sb("s_ktm", [128, CPT * HD], BF16)
                B["vtm"] = sb("s_vtm", [128, CPT * HD], BF16)
                B["utm"] = sb("s_utm", [128, CPT * HD], BF16)
                B["rhotm"] = sb("s_rhotm", [128, CPT * HD], BF16)
                B["fac"] = sb("s_fac", [128, 8 * HD], BF16)
                B["Apl"] = sb("s_Apl", [128, 16 * 128], BF16)
                B["Amx"] = sb("s_Amx", [128, 16 * 128], BF16)
                B["kqb"] = sb("s_kqb", [128, 6 * HD], BF16)
                B["tmpb"] = sb("s_tmpb", [128, HD], BF16)
                B["ukb"] = sb("s_ukb", [128, HD], BF16)
                B["vrprod"] = sb("s_vrprod", [128, 8 * C], BF16)
                B["vrsb"] = sb("s_vrsb", [128, 16], BF16)
                B["sig"] = sb("s_sig", [128, 512], F32)
                B["sigsc"] = sb("s_sigsc", [128, 512], F32)
                B["sigb"] = sb("s_sigb", [128, 512], BF16)
                B["qcrow"] = sb("s_qcrow", [1, HD], BF16)
                B["rowsF"] = sb("s_rowsF", [128, 2 * HD], F32)
                B["onesrF"] = sb("s_onesrF", [1, 128], F32)
                B["rwkvo"] = sb("s_rwkvo", [128, 2 * HD], BF16)
                B["qglo"] = sb("s_qglo", [128, HD], BF16)
                for i in range(4):
                    B[f"pb{i}"] = psm(f"pb{i}", [128, 512])
                B["pxl"] = psm("s_pxl", [128, 1024])
                B["pacc"] = psm("s_pacc", [128, 1024])
                emit_p1(P, nc, io1, B, T, NT, TS, CPT, NCH)
            P.barrier()

            def _collectives():
                P.op("g", lambda e: e.collective_compute(
                    "AllGather", ALU.bypass, replica_groups=[list(range(NCORES))],
                    ins=[ccsig[:, :]], outs=[ccsigG[:, :]]), "ccsigG", ("osig",))
                P.op("g", lambda e: e.collective_compute(
                    "AllGather", ALU.bypass, replica_groups=[list(range(NCORES))],
                    ins=[ccqt[:, :]], outs=[ccqtG[:, :]]), "ccqtG", ("oqt",))
            io2["collectives"] = _collectives
            s2 = ExitStack()
            with s2:
                sb2 = lambda n_, s_, d_: s2.enter_context(nc.sbuf_tensor(n_, s_, d_))
                ps2 = lambda n_, s_, d_=F32: s2.enter_context(nc.psum_tensor(n_, s_, d_))
                B2 = {}
                B2["wfc"] = sb2("f_wfc", [128, 8 * FO], BF16)
                B2["onesc"] = sb2("f_onesc", [128, 4], BF16)
                B2["onesr"] = sb2("f_onesr", [1, 384], BF16)
                B2["onesrF"] = sb2("f_onesrF", [1, 128], F32)
                B2["out1"] = sb2("f_out1", [128, 8 * T], BF16)
                for i in range(6):
                    B2[f"pb{i}"] = ps2(f"f_pb{i}", [128, 512])
                B2["pst"] = ps2("f_pst", [1, 512])
                B2["pst2"] = ps2("f_pst2", [1, 512])
                a2 = ExitStack()
                with a2:
                    sba = lambda n_, s_, d_: a2.enter_context(nc.sbuf_tensor(n_, s_, d_))
                    B2["cmask"] = sba("f_cmask", [128, 16], F32)
                    B2["gate"] = sba("f_gate", [128, 8 * E], BF16)
                    B2["outw"] = sba("f_outw", [128, 8 * E], BF16)
                    B2["outb"] = sba("f_outb", [128, 8], F32)
                    B2["gnrep"] = sba("f_gnrep", [128, 2 * HD], BF16)
                    B2["sigI"] = sba("f_sigI", [128, 512], F32)
                    B2["sigB"] = sba("f_sigB", [128, 512], BF16)
                    B2["xt"] = sba("f_xt", [128, 8 * TS], BF16)
                    B2["rwkvc"] = sba("f_rwkvc", [128, HD], F32)
                    B2["rwkvb"] = [sba("f_rwkvb" + str(i2), [128, HD], BF16)
                                      for i2 in range(2)]
                    B2["qgl"] = [sba("f_qgl" + str(i2), [128, HD], BF16)
                                      for i2 in range(2)]
                    B2["stats"] = sba("f_stats", [128, 128], F32)
                    B2["nrm"] = sba("f_nrm", [128, HD], F32)
                    B2["nrmb"] = sba("f_nrmb", [128, HD], BF16)
                    B2["gsil"] = sba("f_gsil", [128, HD], BF16)
                    B2["prod"] = [sba("f_prod" + str(i2), [128, HD], F32)
                                      for i2 in range(2)]
                    B2["idnF"] = sba("f_idnF", [128, 128], F32)
                    B2["idnB"] = sba("f_idnB", [128, 128], BF16)
                    B2["prodT"] = sba("f_prodT", [128, 8 * TS], BF16)
                    B2["dd5t"] = sba("f_dd5t", [128, 8 * TS], BF16)
                    B2["rfmt"] = sba("f_rfmt", [128, 8 * TS], BF16)
                    B2["vres"] = sba("f_vres", [128, 8 * TS], BF16)
                    B2["rowst"] = sba("f_rowst", [128, TS], F32)
                    P.dma("q", B2["cmask"][:, :], io["cmaskC"][:, :], "cmask")
                    emit_p2_attn(P, nc, io2, B2, T, NTT, TS, CPT, FO)
                P.barrier()
                f2 = ExitStack()
                with f2:
                    sbf = lambda n_, s_, d_: f2.enter_context(nc.sbuf_tensor(n_, s_, d_))
                    B2["wpj"] = sbf("f_wpj", [128, 32 * E], BF16)
                    B2["hbuf"] = sbf("f_hbuf", [128, 32 * FT], BF16)
                    B2["relu"] = sbf("f_relu", [128, FT], F32)
                    B2["sacc"] = sbf("f_sacc", [128, FT], BF16)
                    B2["vres2"] = sbf("f_vres2", [128, 8 * FT], BF16)
                    B2["rowst2"] = sbf("f_rowst2", [128, FT], F32)
                    emit_p2_ffn(P, nc, io2, B2, T, NTT, TS, FT, NFT, FO)
    return nc


def kernel(**inputs):
    x = np.asarray(inputs["L0_attn_x"], np.float32)[0]
    Tfull = x.shape[0]
    Tc = Tfull // NCORES
    shared = prep_shared(inputs)
    xT = np.ascontiguousarray(x.T)
    xTp = np.concatenate([np.zeros((E, 1), np.float32), xT], 1)

    nc = build_fused(Tc)
    in_maps = []
    for cid in range(NCORES):
        m = {k: shared[k] for k in P1_KEYS + P2_KEYS if k in shared}
        m["xT"] = np.ascontiguousarray(xTp[:, cid * Tc: cid * Tc + Tc + 1])
        cm = np.zeros((128, 16), np.float32)
        for j in range(8):
            cm[:, j] = 1.0 if j < cid else 0.0
            cm[:, 8 + j] = 0.0 if j < cid else 1.0
        m["cmaskC"] = cm
        in_maps.append(m)
    res = run_bass_kernel_spmd(nc, in_maps, core_ids=list(range(NCORES)))
    outs = [np.asarray(r["o_out"], np.float32).T for r in res.results]
    return np.ascontiguousarray(np.concatenate(outs, 0))[None, :, :]
